# version 02
# baseline (speedup 1.0000x reference)
import math
import numpy as np
import ml_dtypes
import concourse.bass as bass
import concourse.mybir as mybir
from concourse.bass_utils import run_bass_kernel_spmd

F32 = mybir.dt.float32
BF16 = mybir.dt.bfloat16
I32 = mybir.dt.int32
AF = mybir.ActivationFunctionType
ALU = mybir.AluOpType
AX = mybir.AxisListType

D = 1024
T = 2048
KC = D // 128
NT = T // 128
DI = 2048
NH = 16
HD = 128
NS = 64
NE = 8
PROJ = 4256
EPS1 = 1e-5
EPS2 = 1e-6
NCORES = 8
GRP = 1
TWO_PI = 2.0 * math.pi
RMAGIC = 12582912.0


class Prog:
    ENGINES = ("tensor", "vector", "scalar", "gpsimd", "sync")

    def __init__(self, nc):
        self.nc = nc
        self.plan = {e: [] for e in self.ENGINES}
        self.sems = {}
        self.counts = {}

    def sem(self, name):
        self.sems[name] = self.nc.alloc_semaphore(name=name)
        self.counts[name] = 0

    DMA_SEMS = ("WG", "IN", "OUT")

    def op(self, eng, fn, waits=(), inc=None, amt=1):
        newv = None
        if inc is not None:
            self.counts[inc] += amt
            newv = self.counts[inc]
        wl = []
        seen = {}
        for (s, v) in waits:
            if v and v > seen.get(s, 0):
                seen[s] = v
        wl = [(self.sems[s], v) for s, v in seen.items()]
        isem = self.sems[inc] if inc is not None else None
        ia = amt * 16 if inc in self.DMA_SEMS else amt

        def closure(E, wl=wl, fn=fn, isem=isem, ia=ia):
            for h, v in wl:
                E.wait_ge(h, v)
            ins = fn(E)
            if isem is not None:
                ins.then_inc(isem, ia)
        self.plan[eng].append(closure)
        return newv

    def cnt(self, name):
        return self.counts[name]

    def fence(self):
        return [("T", self.cnt("T")), ("V", self.cnt("V")),
                ("S", self.cnt("S")), ("G", self.cnt("G"))]

    def emit(self):
        with self.nc.Block() as block:
            for eng in self.ENGINES:
                ops = self.plan[eng]
                getattr(block, eng)(lambda E, ops=ops: [f(E) for f in ops])


class Ctx:
    def __init__(self, nc, p, dram):
        self.nc = nc
        self.p = p
        self.dram = dram
        self._scopes = []
        self.dbg = {}
        self.bguard = {}

    def sb(self, name, shape, dtype):
        return self.nc.alloc_sbuf_tensor(name, list(shape), dtype).ap()

    def scoped(self, name, shape, dtype):
        self._uid = getattr(self, "_uid", 0) + 1
        g = self.nc.sbuf_tensor(f"{name}_{self._uid}", list(shape), dtype)
        h = g.__enter__()
        self._scopes.append(g)
        return h.ap()

    def close_scope(self, upto=0):
        while len(self._scopes) > upto:
            g = self._scopes.pop()
            g.__exit__(None, None, None)

    def pb(self, bank, cols=512, off=0):
        assert 0 <= bank < 6 and 512 * bank + off + cols <= 3072
        return self.pA[:, 512 * bank + off: 512 * bank + off + cols]

    def prow(self, bank, cols=512, off=0):
        assert 0 <= bank < 6 and off + cols <= 512
        return self.pA[0:1, 512 * bank + off: 512 * bank + off + cols]

    def pt(self, bank, cols=1024, off=0):
        assert bank in (6, 7) and off + cols <= 1024
        b = bank - 6
        return self.pT[:, 1024 * b + off: 1024 * b + off + cols]

    def guard(self, *banks):
        return [self.bguard[b] for b in banks if b in self.bguard]

    def rguard(self, bank, sem, cnt):
        self.bguard[bank] = (sem, cnt)

    def setup(self):
        p, dram = self.p, self.dram
        self.x = self.sb("x", [128, KC, T], F32)
        self.xbc = self.sb("xbc", [128, KC, 128], BF16)
        self.f32scr = self.sb("f32scr", [128, 512], F32)
        self.RF1 = self.sb("RF1", [128, T], F32)
        self.RB = self.sb("RB", [128, 6144], BF16)
        self.mu_row = self.RF1[0:1, :]
        self.r_row = self.RF1[32:33, :]
        self.tmp_row = self.RF1[64:65, :]
        self.r12_row = self.RF1[96:97, :]
        self.r1p1_row = self.RF1[64:65, :]
        self.mur_row = self.RF1[0:1, :]
        self.mu_b = self.RB[32:33, 0:T]
        self.ir12_b = self.RB[33:34, 0:T]
        self.mu2 = self.RB[32:34, 0:T]
        self.stage_b = self.RB[0:1, :]
        self.idf = self.sb("idf_s", [128, 128], F32)
        self.idb = self.sb("idb", [128, 128], BF16)
        self.tri = self.sb("tri_s", [128, 128], BF16)
        self.onec = self.sb("onec", [128, 1], BF16)
        self.onecf = self.sb("onecf", [128, 1], F32)
        self.onerf = self.sb("onerf", [1, 128], F32)
        self.onerf_all = self.sb("onerf_all", [65, 128], F32)
        self.pA = self.nc.alloc_psum_tensor("pA", [128, 3072], F32).ap()
        self.pT = self.nc.alloc_psum_tensor("pT", [128, 2048], BF16).ap()

        p.op("sync", lambda E: E.dma_start(
            self.x[:], dram["xT"].rearrange("(k p) t -> p k t", p=128)),
            inc="IN")
        self.iv_x = p.cnt("IN") * 16
        for nm, dst in (("ident_f", self.idf),
                        ("ones_col", self.onecf), ("ones_row", self.onerf),
                        ("ones65", self.onerf_all)):
            p.op("sync", lambda E, nm=nm, dst=dst: E.dma_start(
                dst[:], dram[nm]), inc="IN")
        for nm, dst in (("ident_b2", self.idb), ("ones_col", self.onec),
                        ("tri", self.tri)):
            p.op("gpsimd", lambda E, nm=nm, dst=dst: E.dma_start(
                dst[:], dram[nm]), inc="WG")
        self.iv_const = p.cnt("IN") * 16
        self.gv_const = p.cnt("WG") * 16

    def store_out(self):
        p = self.p
        p.op("sync", lambda E: E.dma_start(
            self.dram["out"].rearrange("(k p) t -> p k t", p=128), self.x[:]),
            waits=p.fence(), inc="OUT")
        for nm, ap in self.dbg.items():
            bf = (ap.tensor.dtype == BF16)
            d = self.nc.dram_tensor(
                "dbg_" + nm, list(ap.shape), F32,
                kind="ExternalOutput").ap()
            eng = "gpsimd" if bf else "sync"
            sem = "WG" if bf else "OUT"
            p.op(eng, lambda E, d=d, ap=ap: E.dma_start(d, ap),
                 waits=p.fence(), inc=sem)
        ov = p.cnt("OUT") * 16
        gv = p.cnt("WG") * 16
        p.op("sync", lambda E: E.nop(), waits=[("OUT", ov), ("WG", gv)])


def build_nc(phases=("ffn1", "attn", "ffn2"), dbg=False):
    nc = bass.Bass("TRN2", target_bir_lowering=False,
                   detect_race_conditions=False)
    dram = {}

    def din(name, shape):
        dram[name] = nc.dram_tensor(name, list(shape), F32,
                                    kind="ExternalInput").ap()

    din("xT", [D, T])
    din("w_in", [D, PROJ])
    din("w_theta", [D, 32])
    din("w_out", [DI, D])
    din("wg", [D, NE])
    din("bg_col", [NE, 1])
    din("w1", [NE, D, D])
    din("w2", [NE, D, D])
    din("b1T", [D, NE])
    din("b2", [NE, D])
    din("gam", [128, KC])
    din("bet", [128, KC])
    din("a_log_col", [NH, 1])
    din("dt_bias128", [128, NH])
    din("d128", [128, NH])
    din("bcB128", [128, NS])
    din("bcC128", [128, NS])
    din("bB128", [128, NS])
    din("bC128", [128, NS])
    din("selh", [NH, NH * 128])
    din("sel8", [NE, NE * 128])
    din("ident_f", [128, 128])
    din("ident_b2", [128, 128])
    din("tri", [128, 128])
    din("ones_col", [128, 1])
    din("ones_row", [1, 128])
    din("ones65", [65, 128])
    dram["out"] = nc.dram_tensor("out", [D, T], F32,
                                 kind="ExternalOutput").ap()

    p = Prog(nc)
    for s in ("WG", "IN", "OUT", "T", "V", "S", "G"):
        p.sem(s)
    ctx = Ctx(nc, p, dram)
    ctx.want_dbg = dbg
    ctx.setup()
    if "ffn1" in phases:
        emit_ffn(ctx)
    if "attn" in phases:
        emit_attn(ctx)
    for cname in ("prep", "inproj", "scan", "p1", "p2", "p3", "p4",
                  "p3a", "p3b", "p2b"):
        if ("attn_" + cname) in phases:
            emit_attn(ctx, cut=cname)
    if "ffn2" in phases:
        emit_ffn(ctx)
    ctx.store_out()
    p.emit()
    return nc



def emit_ln_stats(ctx, attn=False):
    p, x = ctx.p, ctx.x
    scr = ctx.lnscr
    fence = p.fence()
    tq = 0
    for tt in range(8):
        sl = slice(256 * tt, 256 * (tt + 1))
        sv = None
        for k in range(KC):
            sv = p.op("scalar", lambda E, k=k, sl=sl: E.copy(
                scr[:, k, :], x[:, k, sl]),
                waits=(fence + [("IN", ctx.iv_x)]) if (tt == 0 and k == 0)
                else ([("T", tq)] if k == 0 else ()), inc="S")
        tv = None
        for k in range(KC):
            tv = p.op("tensor", lambda E, k=k: E.matmul(
                ctx.prow(4, 256), ctx.onec[:, 0:1], scr[:, k, :],
                start=(k == 0), stop=(k == KC - 1)),
                waits=([("S", sv), ("WG", ctx.gv_const)] + ctx.guard(4)
                       if k == 0 else ()),
                inc="T" if k == KC - 1 else None)
        sv = None
        for k in range(KC):
            sv = p.op("scalar", lambda E, k=k, sl=sl: E.activation(
                scr[:, k, :], x[:, k, sl], AF.Square),
                waits=[("T", tv)] if k == 0 else (), inc="S")
        for k in range(KC):
            tq = p.op("tensor", lambda E, k=k: E.matmul(
                ctx.prow(5, 256), ctx.onec[:, 0:1], scr[:, k, :],
                start=(k == 0), stop=(k == KC - 1)),
                waits=([("S", sv)] + ctx.guard(5)) if k == 0 else (),
                inc="T" if k == KC - 1 else None)
        a = 256 * tt
        v4 = p.op("vector", lambda E, a=a: E.tensor_scalar(
            ctx.mu_row[0:1, a:a + 256], ctx.prow(4, 256), 1.0 / D, None,
            ALU.mult), waits=[("T", tv)], inc="V")
        v5 = p.op("vector", lambda E, a=a: E.tensor_scalar(
            ctx.tmp_row[0:1, a:a + 256], ctx.prow(5, 256), 1.0 / D, None,
            ALU.mult), waits=[("T", tq)], inc="V")
        ctx.rguard(4, "V", v4)
        ctx.rguard(5, "V", v5)
    scr64 = ctx.f32scr[64:65, 0:512]
    for a in range(0, T, 512):
        p.op("vector", lambda E, a=a: E.tensor_tensor(
            scr64, ctx.mu_row[0:1, a:a + 512], ctx.mu_row[0:1, a:a + 512],
            ALU.mult), inc="V")
        p.op("vector", lambda E, a=a: E.tensor_tensor(
            ctx.tmp_row[0:1, a:a + 512], ctx.tmp_row[0:1, a:a + 512],
            scr64, ALU.subtract), inc="V")
    vv = p.op("vector", lambda E: E.tensor_scalar(
        ctx.r12_row[0:1, :], ctx.tmp_row[0:1, :], EPS1, None, ALU.add),
        inc="V")
    p.op("scalar", lambda E: E.activation(
        ctx.r_row[0:1, :], ctx.r12_row[0:1, :], AF.Ln),
        waits=[("V", vv)], inc="S")
    sv = p.op("scalar", lambda E: E.activation(
        ctx.r_row[0:1, :], ctx.r_row[0:1, :], AF.Exp, scale=-0.5), inc="S")
    ctx.ln_done = [("S", sv)]
    if not attn:
        return
    for a in range(0, T, 512):
        p.op("vector", lambda E, a=a: E.tensor_tensor(
            scr64, ctx.r_row[0:1, a:a + 512], ctx.r_row[0:1, a:a + 512],
            ALU.mult), waits=[("S", sv)] if a == 0 else (), inc="V")
        vv = p.op("vector", lambda E, a=a: E.tensor_tensor(
            ctx.r12_row[0:1, a:a + 512], ctx.tmp_row[0:1, a:a + 512],
            scr64, ALU.mult), inc="V")
    vv = p.op("vector", lambda E: E.tensor_scalar(
        ctx.r12_row[0:1, :], ctx.r12_row[0:1, :], EPS2, None, ALU.add),
        inc="V")
    p.op("scalar", lambda E: E.activation(
        ctx.r12_row[0:1, :], ctx.r12_row[0:1, :], AF.Ln),
        waits=[("V", vv)], inc="S")
    sv = p.op("scalar", lambda E: E.activation(
        ctx.r12_row[0:1, :], ctx.r12_row[0:1, :], AF.Exp, scale=-0.5),
        inc="S")
    scr96 = ctx.f32scr[96:97, 0:512]
    for a in range(0, T, 512):
        p.op("vector", lambda E, a=a: E.tensor_copy(
            scr96, ctx.r_row[0:1, a:a + 512]),
            waits=[("S", sv)] if a == 0 else (), inc="V")
        vv = p.op("vector", lambda E, a=a: E.tensor_tensor(
            ctx.r12_row[0:1, a:a + 512], ctx.r12_row[0:1, a:a + 512],
            scr96, ALU.mult), inc="V")
    p.op("vector", lambda E: E.tensor_scalar(
        ctx.mu_b[0:1, :], ctx.mu_row[0:1, :], -1.0, None, ALU.mult), inc="V")
    p.op("vector", lambda E: E.reciprocal(
        ctx.tmp_row[0:1, :], ctx.r12_row[0:1, :]), waits=[("V", vv)],
        inc="V")
    vv = p.op("vector", lambda E: E.tensor_copy(
        ctx.stage_b[0:1, 0:T], ctx.tmp_row[0:1, :]), inc="V")
    p.op("sync", lambda E: E.dma_start(
        ctx.ir12_b[0:1, :], ctx.stage_b[0:1, 0:T]),
        waits=[("V", vv)], inc="IN")
    ctx.ir12_dma = p.cnt("IN") * 16
    ctx.ln_done = [("V", vv), ("IN", ctx.ir12_dma)]



def emit_ffn(ctx):
    p, dram, x = ctx.p, ctx.dram, ctx.x
    bsc = len(ctx._scopes)
    ctx.lnscr = ctx.RB[:, 0:2048].rearrange("p (k c) -> p k c", c=256)
    t1 = ctx.scoped("t1", [128, KC, 1024], BF16)
    hsb = ctx.scoped("hsb", [128, KC, 1024], BF16)
    yac = ctx.scoped("yac", [128, KC, 1024], BF16)
    w1b = ctx.scoped("w1b", [128, KC, D], BF16)
    w2b = ctx.scoped("w2b", [128, KC, D], BF16)
    wgb = ctx.scoped("wgb", [128, KC, NE], BF16)
    b1s = ctx.scoped("b1s", [128, KC, NE], F32)
    b2b = ctx.scoped("b2b", [NE, D], BF16)
    bgc = ctx.scoped("bgc", [NE, 1], F32)
    sel8b = ctx.scoped("sel8b", [NE, NE * 128], BF16)
    logF = ctx.scoped("logF", [NE, 1024], F32)
    ltm = ctx.scoped("ltm", [128, 8, NE], F32)
    lmax = ctx.scoped("lmax", [128, 8], F32)
    mtm = ctx.scoped("mtm", [128, 8, NE], BF16)
    mF = ctx.scoped("mF", [NE, 1024], BF16)
    mbc = ctx.RB[:, 2048:3072]

    fence_ffn = ctx.p.fence()
    emit_ln_stats(ctx, attn=False)
    lnw = ctx.ln_done

    p.op("gpsimd", lambda E: E.dma_start(
        wgb[:], dram["wg"].rearrange("(k p) e -> p k e", p=128)),
        waits=fence_ffn, inc="WG")
    p.op("gpsimd", lambda E: E.dma_start(b2b[:], dram["b2"]), inc="WG")
    p.op("gpsimd", lambda E: E.dma_start(sel8b[:], dram["sel8"]), inc="WG")
    gv = p.cnt("WG") * 16
    p.op("sync", lambda E: E.dma_start(
        b1s[:], dram["b1T"].rearrange("(k p) e -> p k e", p=128)),
        waits=fence_ffn, inc="IN")
    p.op("sync", lambda E: E.dma_start(bgc[:], dram["bg_col"]), inc="IN")
    iv = p.cnt("IN") * 16

    def load_w(wt, dst, e, tdep):
        for k in range(KC):
            p.op("gpsimd", lambda E, k=k, e=e, wt=wt, dst=dst: E.dma_start(
                dst[:, k, :], dram[wt][e, 128 * k:128 * (k + 1), :]),
                waits=tdep if k == 0 else (), inc="WG")
        return p.cnt("WG") * 16

    w1v = load_w("w1", w1b, 0, ())
    w2v = load_w("w2", w2b, 0, ())

    for th in range(2):
        for tts in range(2):
            a = 1024 * th + 512 * tts
            p.op("tensor", lambda E, a=a: E.matmul(
                ctx.pb(4), ctx.onerf_all[0:1, :],
                ctx.mu_row[0:1, a:a + 512],
                start=True, stop=True),
                waits=(lnw + ctx.guard(4, 5)), inc="T")
            tv = p.op("tensor", lambda E, a=a: E.matmul(
                ctx.pb(5), ctx.onerf_all[32:33, :],
                ctx.r_row[0:1, a:a + 512],
                start=True, stop=True), inc="T")
            vv = None
            for k in range(KC):
                p.op("vector", lambda E, k=k, a=a: E.tensor_tensor(
                    ctx.f32scr[:, 0:512], x[:, k, a:a + 512], ctx.pb(4),
                    ALU.subtract), waits=[("T", tv)] if k == 0 else (),
                    inc="V")
                vv = p.op("vector", lambda E, k=k, tts=tts: E.tensor_tensor(
                    t1[:, k, 512 * tts:512 * (tts + 1)],
                    ctx.f32scr[:, 0:512], ctx.pb(5), ALU.mult), inc="V")
            ctx.rguard(4, "V", vv)
            ctx.rguard(5, "V", vv)
        t1_ready = p.cnt("V")

        for tts in range(2):
            tv = None
            for k in range(KC):
                tv = p.op("tensor", lambda E, k=k, tts=tts: E.matmul(
                    ctx.pA[0:NE, 512 * 4:512 * 4 + 512], wgb[:, k, :],
                    t1[:, k, 512 * tts:512 * (tts + 1)],
                    start=(k == 0), stop=(k == KC - 1)),
                    waits=([("V", t1_ready), ("WG", gv), ("IN", iv)]
                           + ctx.guard(4) if k == 0 else ()),
                    inc="T" if k == KC - 1 else None)
            vv = p.op("vector", lambda E, tts=tts: E.tensor_scalar(
                logF[:, 512 * tts:512 * (tts + 1)],
                ctx.pA[0:NE, 512 * 4:512 * 4 + 512], bgc[0:NE, 0:1], None,
                ALU.add), waits=[("T", tv)], inc="V")
            ctx.rguard(4, "V", vv)
        logv = p.cnt("V")
        tv = None
        for c8 in range(8):
            tv = p.op("tensor", lambda E, c8=c8: E.transpose(
                ctx.pA[:, 512 * 4 + 8 * c8: 512 * 4 + 8 * (c8 + 1)],
                logF[:, 128 * c8:128 * (c8 + 1)], ctx.idf[0:NE, 0:NE]),
                waits=([("V", logv)] + ctx.guard(4)) if c8 == 0 else (),
                inc="T" if c8 == 7 else None)
        vv = p.op("vector", lambda E: E.tensor_copy(
            ltm.rearrange("p c e -> p (c e)"),
            ctx.pA[:, 512 * 4:512 * 4 + 64]), waits=[("T", tv)], inc="V")
        ctx.rguard(4, "V", vv)
        vv = p.op("vector", lambda E: E.tensor_reduce(
            lmax[:, :], ltm[:, :, :], AX.X, ALU.max), waits=[("V", vv)],
            inc="V")
        mv = None
        for c8 in range(8):
            mv = p.op("vector", lambda E, c8=c8: E.tensor_scalar(
                mtm[:, c8, :], ltm[:, c8, :], lmax[:, c8:c8 + 1], None,
                ALU.is_ge), waits=[("V", vv)] if c8 == 0 else (), inc="V")
        tv = None
        for c8 in range(8):
            tv = p.op("tensor", lambda E, c8=c8: E.transpose(
                ctx.pt(6, 128, off=128 * c8)[0:NE, :], mtm[:, c8, :],
                ctx.idb[:]),
                waits=([("V", mv)] + ctx.guard(6)) if c8 == 0 else (),
                inc="T" if c8 == 7 else None)
        maskv = p.op("vector", lambda E: E.tensor_copy(
            mF[:, :], ctx.pt(6)[0:NE, :]), waits=[("T", tv)], inc="V")
        ctx.rguard(6, "V", maskv)
        yac_v = p.op("vector", lambda E: E.memset(
            yac.rearrange("p k t -> p (k t)"), 0.0), inc="V")

        for e in range(NE):
            for tts in range(2):
                tv = p.op("tensor", lambda E, e=e, tts=tts: E.matmul(
                    ctx.pb(4), sel8b[:, 128 * e:128 * (e + 1)],
                    mF[0:NE, 512 * tts:512 * (tts + 1)],
                    start=True, stop=True),
                    waits=([("V", maskv), ("WG", gv)] + ctx.guard(4)),
                    inc="T")
                vm = p.op("vector", lambda E, tts=tts: E.tensor_copy(
                    mbc[:, 512 * tts:512 * (tts + 1)], ctx.pb(4)),
                    waits=[("T", tv)], inc="V")
                ctx.rguard(4, "V", vm)
            mbcv = p.cnt("V")
            hv = None
            for ht in range(KC):
                bk = (ht % 2) * 2
                tv = None
                for tts in range(2):
                    for k in range(KC):
                        w = []
                        if k == 0:
                            w = ctx.guard(bk + tts)
                            if ht == 0 and tts == 0:
                                w = w + [("WG", w1v), ("V", t1_ready)]
                        tv = p.op("tensor", lambda E, k=k, ht=ht, tts=tts,
                                  bk=bk: E.matmul(
                            ctx.pb(bk + tts),
                            w1b[:, k, 128 * ht:128 * (ht + 1)],
                            t1[:, k, 512 * tts:512 * (tts + 1)],
                            start=(k == 0), stop=(k == KC - 1)),
                            waits=w, inc="T" if k == KC - 1 else None)
                sv = p.op("scalar", lambda E, ht=ht, bk=bk, e=e: E.activation(
                    hsb[:, ht, :], ctx.pb(bk, 1024), AF.Gelu,
                    bias=b1s[:, ht, e:e + 1]),
                    waits=[("T", tv), ("IN", iv)], inc="S")
                ctx.rguard(bk, "S", sv)
                ctx.rguard(bk + 1, "S", sv)
                hv = p.op("vector", lambda E, ht=ht: E.tensor_tensor(
                    hsb[:, ht, :], hsb[:, ht, :], mbc[:, :], ALU.mult),
                    waits=[("S", sv), ("V", mbcv)], inc="V")
            nxt = (e + 1) if e < NE - 1 else (0 if th == 0 else None)
            if nxt is not None:
                w1v = load_w("w1", w1b, nxt, [("T", p.cnt("T"))])
            for do in range(KC):
                bk = (do % 2) * 2
                tv = None
                for tts in range(2):
                    for i in range(KC):
                        w = []
                        if i == 0:
                            w = ctx.guard(bk + tts)
                            if do == 0 and tts == 0:
                                w = w + [("V", hv), ("V", yac_v),
                                         ("WG", w2v)]
                        tv = p.op("tensor", lambda E, i=i, do=do, tts=tts,
                                  bk=bk: E.matmul(
                            ctx.pb(bk + tts),
                            w2b[:, i, 128 * do:128 * (do + 1)],
                            hsb[:, i, 512 * tts:512 * (tts + 1)],
                            start=(i == 0), stop=(i == KC - 1)),
                            waits=w, inc="T" if i == KC - 1 else None)
                vv = p.op("vector", lambda E, do=do, bk=bk: E.tensor_tensor(
                    yac[:, do, :], yac[:, do, :], ctx.pb(bk, 1024), ALU.add),
                    waits=[("T", tv)], inc="V")
                ctx.rguard(bk, "V", vv)
                ctx.rguard(bk + 1, "V", vv)
            nxt = (e + 1) if e < NE - 1 else (0 if th == 0 else None)
            if nxt is not None:
                w2v = load_w("w2", w2b, nxt, [("T", p.cnt("T"))])
        yv = p.cnt("V")
        for do in range(KC):
            tvs = []
            for tts in range(2):
                tvs.append(p.op("tensor", lambda E, do=do, tts=tts: E.matmul(
                    ctx.pb(4 + tts), b2b[0:NE, 128 * do:128 * (do + 1)],
                    mF[0:NE, 512 * tts:512 * (tts + 1)],
                    start=True, stop=True),
                    waits=([("V", yv)] + ctx.guard(4 + tts)), inc="T"))
            for tts in range(2):
                a = 1024 * th + 512 * tts
                p.op("vector", lambda E, do=do, tts=tts:
                     E.scalar_tensor_tensor(
                    ctx.f32scr[:, 0:512],
                    ctx.pb(4 + tts), 1.0,
                    yac[:, do, 512 * tts:512 * (tts + 1)],
                    ALU.bypass, ALU.add), waits=[("T", tvs[tts])], inc="V")
                vv = p.op("vector", lambda E, do=do, a=a, tts=tts:
                          E.scalar_tensor_tensor(
                    x[:, do, a:a + 512],
                    ctx.f32scr[:, 0:512], 0.5,
                    x[:, do, a:a + 512], ALU.mult, ALU.add), inc="V")
                ctx.rguard(4 + tts, "V", vv)
    ctx.close_scope(bsc)



def emit_attn(ctx, cut=None):
    p, dram, x = ctx.p, ctx.dram, ctx.x
    bsc = len(ctx._scopes)
    fence0 = p.fence()

    FM = ctx.scoped("FM", [128, T], F32)
    selhf = ctx.scoped("selhf", [NH, NH * 128], F32)
    csxz = ctx.RB[32:33, 2048:6144]
    bwxz = ctx.RB[33:34, 2048:6144]
    csbw = ctx.RB[32:34, 2048:6144]
    gcol = ctx.scoped("gcol", [128, KC], BF16)
    bcol = ctx.scoped("bcol", [128, KC], BF16)
    gcolf = ctx.scoped("gcolf", [128, KC], F32)
    lncols = ctx.scoped("lncols", [128, NT, 1], F32)
    acol = ctx.scoped("acol", [NH, 1], F32)
    nacol = ctx.scoped("nacol", [NH, 1], F32)
    dlt = ctx.scoped("dlt", [128, NT, NH], F32)
    dEs = ctx.scoped("dEs", [128, NT, NH], F32)
    coef = ctx.scoped("coef", [128, NT, NH], F32)
    btm = ctx.scoped("btm", [128, NT, NS], BF16)
    ctm = ctx.scoped("ctm", [128, NT, NS], BF16)
    tmE2 = ctx.scoped("tmE2", [128, NT, 16], F32)
    tmNR = ctx.scoped("tmNR", [128, NT, 16], F32)
    dcol128 = ctx.scoped("dcol128", [128, NH], F32)
    a128 = ctx.scoped("a128", [128, NH], F32)
    na128 = ctx.scoped("na128", [128, NH], F32)
    if getattr(ctx, "want_dbg", False):
        ctx.dbg.update(FM=FM, dlt=dlt, dEs=dEs, coef=coef, btm=btm,
                       ctm=ctm, tmE2=tmE2, tmNR=tmNR, lncols=lncols,
                       a128=a128, RF1=ctx.RF1, RB=ctx.RB)
    psc = len(ctx._scopes)
    lnscr_a = ctx.scoped("lnscr_a", [128, KC, 256], BF16)
    ctx.lnscr = lnscr_a
    relF = FM[0:16, :]
    thetaF = FM[32:64, :]
    dtFv = FM[64:80, :]
    gamFv = FM[96:112, :]
    r1p1_row = ctx.r1p1_row
    mur_row = ctx.mur_row

    emit_ln_stats(ctx, attn=True)
    lnw = ctx.ln_done

    for nm, dst in (("gam", None), ("bet", None)):
        pass
    p.op("gpsimd", lambda E: E.dma_start(gcol[:], dram["gam"]),
        waits=fence0, inc="WG")
    p.op("gpsimd", lambda E: E.dma_start(bcol[:], dram["bet"]), inc="WG")
    p.op("sync", lambda E: E.dma_start(gcolf[:], dram["gam"]),
        waits=fence0, inc="IN")
    p.op("sync", lambda E: E.dma_start(acol[:], dram["a_log_col"]), inc="IN")
    p.op("sync", lambda E: E.dma_start(dcol128[:], dram["d128"]), inc="IN")
    p.op("sync", lambda E: E.dma_start(selhf[:, :], dram["selh"]), inc="IN")
    iv_ab = p.cnt("IN") * 16
    sv = p.op("scalar", lambda E: E.activation(nacol[:], acol[:], AF.Exp),
              waits=[("IN", iv_ab)], inc="S")
    av = p.op("vector", lambda E: E.tensor_scalar(
        acol[:], nacol[:], -1.0, None, ALU.mult), waits=[("S", sv)], inc="V")
    tv = p.op("tensor", lambda E: E.transpose(
        ctx.pb(2, NH)[0:1, :], acol[0:NH, 0:1], ctx.idf[0:NH, 0:NH]),
        waits=[("V", av)] + ctx.guard(2), inc="T")
    avv = p.op("vector", lambda E: E.tensor_copy(
        ctx.f32scr[0:1, 384:400], ctx.pb(2, NH)[0:1, :]),
        waits=[("T", tv)], inc="V")
    tv = p.op("tensor", lambda E: E.matmul(
        ctx.pb(2, NH, off=128), ctx.onerf_all[0:1, :],
        ctx.f32scr[0:1, 384:400], start=True, stop=True),
        waits=[("V", avv)], inc="T")
    p.op("vector", lambda E: E.tensor_copy(
        a128[:, :], ctx.pb(2, NH, off=128)), waits=[("T", tv)], inc="V")
    av = p.op("vector", lambda E: E.tensor_scalar(
        na128[:, :], a128[:, :], -1.0, None, ALU.mult), inc="V")
    ctx.rguard(2, "V", av)
    selsc = p.op("vector", lambda E: E.tensor_scalar(
        selhf[:, :], selhf[:, :], nacol[0:NH, 0:1], None, ALU.mult),
        waits=[("IN", iv_ab), ("S", sv)], inc="V")

    vv = p.op("vector", lambda E: E.tensor_copy(
        ctx.tmp_row[0:1, :], ctx.r12_row[0:1, :]), waits=lnw, inc="V")
    for c in range(NT):
        bk = 2 + (c % 2)
        tv = p.op("tensor", lambda E, c=c, bk=bk: E.matmul(
            ctx.pb(bk, 1), ctx.tmp_row[0:1, 128 * c:128 * (c + 1)],
            ctx.onerf_all[64:65, 0:1], start=True, stop=True),
            waits=([("V", vv)] + ctx.guard(bk)), inc="T")
        lv = p.op("vector", lambda E, c=c, bk=bk: E.tensor_copy(
            lncols[:, c, 0:1], ctx.pb(bk, 1)), waits=[("T", tv)], inc="V")
        ctx.rguard(bk, "V", lv)
    lncols_v = p.cnt("V")

    if cut == "p1":
        ctx.close_scope(psc)
        return
    wsm = ctx.scoped("wsm", [128, KC, 160], BF16)
    thwb = ctx.scoped("thwb", [128, KC, 32], BF16)
    smcorr = ctx.scoped("smcorr", [34, 192], BF16)
    thcs = smcorr[32:33, 160:192]
    cs_sm = smcorr[32:33, 0:160]
    bw_sm = smcorr[33:34, 0:160]
    sm2 = smcorr[32:34, 0:160]
    small = ctx.scoped("small", [128, NT, 160], F32)
    dtt = ctx.scoped("dtt", [128, NT, NH], F32)
    gmt = ctx.scoped("gmt", [128, NT, NH], F32)
    alp = ctx.scoped("alp", [128, NT, NH], F32)
    tmAt = ctx.scoped("tmAt", [128, NT, 32], F32)
    tmB = ctx.scoped("tmB", [128, NT, 64], BF16)
    FM2 = ctx.scoped("FM2", [128, T], F32)
    ones64 = FM2[64:80, :]
    cumF = FM2[32:48, :]
    phiF = FM2[64:96, :]
    rscA = FM[32:64, 0:512]
    th64 = FM[32:64, 512:1024]
    sinb = FM[32:64, 1024:1536]
    packF = ctx.scoped("packF", [128, 512], F32)
    packT = ctx.scoped("packT", [64, 512], F32)
    ctx._pf = packF
    bsf = ctx.scoped("bsf", [128, NT, NS], F32)
    csf = ctx.scoped("csf", [128, NT, NS], F32)
    dtb128 = ctx.scoped("dtb128", [128, NH], F32)
    bcB = ctx.scoped("bcB", [128, NS], F32)
    bcC = ctx.scoped("bcC", [128, NS], F32)
    bBt = ctx.scoped("bBt", [128, NS], F32)
    bCt = ctx.scoped("bCt", [128, NS], F32)
    colscr = ctx.scoped("colscr", [128, NT], F32)
    colscr2 = ctx.scoped("colscr2", [128, NT], F32)
    if getattr(ctx, "want_dbg", False):
        ctx.dbg.update(small=small, smcorr=smcorr, bsf=bsf, csf=csf,
                       tmB=tmB, tmAt=tmAt, FM2=FM2, packF=packF, packT=packT)

    p.op("gpsimd", lambda E: E.dma_start(
        wsm[:], dram["w_in"].rearrange("(k p) o -> p k o", p=128)
        [:, :, 4096:4256]), inc="WG")
    p.op("gpsimd", lambda E: E.dma_start(
        thwb[:], dram["w_theta"].rearrange("(k p) o -> p k o", p=128)),
        inc="WG")
    gv_w = p.cnt("WG") * 16
    for nm, dst in (("dt_bias128", dtb128), ("bcB128", bcB), ("bcC128", bcC),
                    ("bB128", bBt), ("bC128", bCt)):
        p.op("sync", lambda E, nm=nm, dst=dst: E.dma_start(dst[:], dram[nm]),
             inc="IN")
    iv_sm = p.cnt("IN") * 16

    for k in range(KC):
        p.op("tensor", lambda E, k=k: E.matmul(
            ctx.prow(2, 160), gcol[:, k:k + 1], wsm[:, k, :],
            start=(k == 0), stop=(k == KC - 1)),
            waits=([("WG", gv_w), ("IN", iv_ab)] + ctx.guard(2))
            if k == 0 else ())
    tv = None
    for k in range(KC):
        tv = p.op("tensor", lambda E, k=k: E.matmul(
            ctx.prow(3, 160), bcol[:, k:k + 1], wsm[:, k, :],
            start=(k == 0), stop=(k == KC - 1)),
            waits=ctx.guard(3) if k == 0 else (),
            inc="T" if k == KC - 1 else None)
    v2 = p.op("vector", lambda E: E.tensor_copy(
        cs_sm[0:1, :], ctx.prow(2, 160)), waits=[("T", tv)], inc="V")
    ctx.rguard(2, "V", v2)
    v3 = p.op("vector", lambda E: E.tensor_copy(
        ctx.stage_b[0:1, 0:160], ctx.prow(3, 160)),
        waits=[("IN", ctx.ir12_dma)], inc="V")
    ctx.rguard(3, "V", v3)
    p.op("sync", lambda E: E.dma_start(
        bw_sm[0:1, :], ctx.stage_b[0:1, 0:160]),
        waits=[("V", v3)], inc="IN")
    sm_dma = p.cnt("IN") * 16
    gsv = None
    for k in range(KC):
        gsv = p.op("vector", lambda E, k=k: E.tensor_scalar(
            wsm[:, k, :], wsm[:, k, :], gcolf[:, k:k + 1], None, ALU.mult),
            waits=[("V", v3), ("IN", p.cnt("IN") * 16)] if k == 0 else (),
            inc="V")

    tv = None
    for k in range(KC):
        tv = p.op("tensor", lambda E, k=k: E.matmul(
            ctx.prow(2, 32), ctx.onec[:, 0:1], thwb[:, k, :],
            start=(k == 0), stop=(k == KC - 1)),
            waits=([("WG", gv_w)] + ctx.guard(2)) if k == 0 else (),
            inc="T" if k == KC - 1 else None)
    thv = p.op("vector", lambda E: E.tensor_copy(
        thcs[0:1, :], ctx.prow(2, 32)), waits=[("T", tv)], inc="V")
    ctx.rguard(2, "V", thv)
    th_tv = 0
    for tt in range(8):
        a = 256 * tt
        sv = None
        for k in range(KC):
            sv = p.op("scalar", lambda E, k=k, a=a: E.copy(
                lnscr_a[:, k, :], x[:, k, a:a + 256]),
                waits=p.fence() if (tt == 0 and k == 0) else (
                    [("T", th_tv)] if k == 0 else ()), inc="S")
        for k in range(KC):
            p.op("tensor", lambda E, k=k: E.matmul(
                ctx.pA[0:32, 512 * 2:512 * 2 + 256], thwb[:, k, :],
                lnscr_a[:, k, :], start=(k == 0), stop=False),
                waits=([("S", sv), ("V", thv)] + ctx.guard(2))
                if k == 0 else ())
        tv = p.op("tensor", lambda E, a=a: E.matmul(
            ctx.pA[0:32, 512 * 2:512 * 2 + 256], thcs[0:1, :],
            ctx.mu_b[0:1, a:a + 256], start=False, stop=True), inc="T")
        th_tv = tv
        vv = p.op("vector", lambda E, a=a: E.tensor_copy(
            thetaF[:, a:a + 256], ctx.pA[0:32, 512 * 2:512 * 2 + 256]),
            waits=[("T", tv)], inc="V")
        ctx.rguard(2, "V", vv)
    thetaF_v = p.cnt("V")

    smv = None
    xbc_war = 0
    for c in range(NT):
        sv = None
        for k in range(KC):
            sv = p.op("scalar", lambda E, k=k, c=c: E.copy(
                ctx.xbc[:, k, :], x[:, k, 128 * c:128 * (c + 1)]),
                waits=[("T", xbc_war)] if k == 0 else (), inc="S")
        for k in range(KC):
            p.op("tensor", lambda E, k=k, c=c: E.matmul(
                ctx.pb(2, 160), ctx.xbc[:, k, :], wsm[:, k, :],
                start=(k == 0), stop=False),
                waits=([("S", sv), ("V", gsv), ("V", v2),
                        ("IN", sm_dma)] + ctx.guard(2))
                if k == 0 else ())
        tv = p.op("tensor", lambda E, c=c: E.matmul(
            ctx.pb(2, 160), ctx.mu2[:, 128 * c:128 * (c + 1)],
            sm2[:, :], start=False, stop=True), inc="T")
        xbc_war = tv
        smv = p.op("scalar", lambda E, c=c: E.activation(
            small[:, c, :], ctx.pb(2, 160), AF.Identity,
            scale=lncols[:, c, 0:1]),
            waits=[("T", tv)] + ([("V", lncols_v)] if c == 0 else []),
            inc="S")
        ctx.rguard(2, "S", smv)

    avv = None
    for c in range(NT):
        avv = p.op("vector", lambda E, c=c: E.tensor_tensor(
            alp[:, c, :], small[:, c, 128:144], dtb128[:, :], ALU.add),
            waits=[("S", smv), ("IN", iv_sm)] if c == 0 else (), inc="V")
    sv = p.op("scalar", lambda E: E.activation(
        dtt.rearrange("p c h -> p (c h)"), alp.rearrange("p c h -> p (c h)"),
        AF.Exp), waits=[("V", avv)], inc="S")
    dt_v = p.op("scalar", lambda E: E.activation(
        dtt.rearrange("p c h -> p (c h)"), dtt.rearrange("p c h -> p (c h)"),
        AF.Ln, bias=1.0), inc="S")
    lam_v = p.op("scalar", lambda E: E.activation(
        gmt[:, :, :], small[:, :, 144:160], AF.Sigmoid),
        waits=[("S", smv)], inc="S")
    vv = p.op("vector", lambda E: E.tensor_tensor(
        alp.rearrange("p c h -> p (c h)"), dtt.rearrange("p c h -> p (c h)"),
        gmt.rearrange("p c h -> p (c h)"), ALU.mult),
        waits=[("S", dt_v), ("S", lam_v)], inc="V")
    gam_v = p.op("vector", lambda E: E.tensor_tensor(
        gmt.rearrange("p c h -> p (c h)"), dtt.rearrange("p c h -> p (c h)"),
        alp.rearrange("p c h -> p (c h)"), ALU.subtract), inc="V")

    for c in range(NT):
        bk = 2 + (c % 2)
        tv = p.op("tensor", lambda E, c=c, bk=bk: E.transpose(
            ctx.pA[0:NH, 512 * bk:512 * bk + 128], dtt[:, c, :],
            ctx.idf[:]),
            waits=([("S", dt_v)] + ctx.guard(bk)), inc="T")
        vv = p.op("vector", lambda E, c=c, bk=bk: E.tensor_copy(
            dtFv[:, 128 * c:128 * (c + 1)],
            ctx.pA[0:NH, 512 * bk:512 * bk + 128]),
            waits=[("T", tv)], inc="V")
        ctx.rguard(bk, "V", vv)
        bk = 2 + ((c + 1) % 2)
        tv = p.op("tensor", lambda E, c=c, bk=bk: E.transpose(
            ctx.pA[0:NH, 512 * bk:512 * bk + 128], gmt[:, c, :],
            ctx.idf[:]),
            waits=([("V", gam_v)] + ctx.guard(bk)), inc="T")
        vv = p.op("vector", lambda E, c=c, bk=bk: E.tensor_copy(
            gamFv[:, 128 * c:128 * (c + 1)],
            ctx.pA[0:NH, 512 * bk:512 * bk + 128]),
            waits=[("T", tv)], inc="V")
        ctx.rguard(bk, "V", vv)
    dgF_v = p.cnt("V")

    if cut == "p2":
        ctx.close_scope(psc)
        return
    vv = p.op("vector", lambda E: E.memset(ones64[:, :], 1.0), inc="V")
    cum_v = p.op("vector", lambda E: E.tensor_tensor_scan(
        cumF[:, :], ones64[0:NH, :], dtFv[:, :], 0.0, ALU.mult, ALU.add),
        waits=[("V", dgF_v), ("V", vv)], inc="V")
    rel_v = p.op("vector", lambda E: E.tensor_copy(
        relF[:, 0:128], cumF[:, 0:128]), waits=[("V", cum_v)], inc="V")
    for c in range(1, NT):
        rel_v = p.op("vector", lambda E, c=c: E.tensor_scalar(
            relF[:, 128 * c:128 * (c + 1)], cumF[:, 128 * c:128 * (c + 1)],
            cumF[:, 128 * c - 1:128 * c], None, ALU.subtract), inc="V")


    dtbv = []
    for tt in range(8):
        vv = p.op("vector", lambda E, tt=tt: E.tensor_copy(
            lnscr_a[0:NH, tt, 0:256], dtFv[:, 256 * tt:256 * (tt + 1)]),
            waits=[("V", dgF_v)], inc="V")
        dtbv.append(vv)
    for tt in range(8):
        a = 256 * tt
        tv = p.op("tensor", lambda E, tt=tt: E.matmul(
            ctx.prow(2, 256), ctx.onec[0:NH, 0:1],
            lnscr_a[0:NH, tt, 0:256],
            start=True, stop=True),
            waits=([("V", dtbv[tt])] + ctx.guard(2)), inc="T")
        vv = p.op("vector", lambda E, a=a: E.tensor_scalar(
            ctx.tmp_row[0:1, a:a + 256], ctx.prow(2, 256), 1.0 / NH, None,
            ALU.mult), waits=[("T", tv)], inc="V")
        ctx.rguard(2, "V", vv)
    dts_v = None
    for a in range(0, T, 512):
        p.op("vector", lambda E, a=a: E.tensor_copy(
            ctx.f32scr[64:65, 0:512], ctx.r_row[0:1, a:a + 512]), inc="V")
        dts_v = p.op("vector", lambda E, a=a: E.tensor_tensor(
            ctx.tmp_row[0:1, a:a + 512], ctx.tmp_row[0:1, a:a + 512],
            ctx.f32scr[64:65, 0:512], ALU.mult), inc="V")
    for tt in range(4):
        a = 512 * tt
        tv = p.op("tensor", lambda E, a=a: E.matmul(
            ctx.pA[0:32, 512 * 2:512 * 2 + 512],
            ctx.onerf_all[64:65, 0:32],
            ctx.tmp_row[0:1, a:a + 512], start=True, stop=True),
            waits=([("V", dts_v)] + ctx.guard(2)), inc="T")
        vv = p.op("vector", lambda E, a=a: E.tensor_tensor(
            thetaF[:, a:a + 512], thetaF[:, a:a + 512],
            ctx.pA[0:32, 512 * 2:512 * 2 + 512], ALU.mult),
            waits=[("T", tv), ("V", thetaF_v)], inc="V")
        ctx.rguard(2, "V", vv)
    if cut == "p2b":
        ctx.close_scope(psc)
        return
    ones32 = FM2[32:64, :]
    vv = p.op("vector", lambda E: E.memset(ones32[:, :], 1.0), inc="V")
    phi_v = p.op("vector", lambda E: E.tensor_tensor_scan(
        phiF[:, :], ones32[:, :], thetaF[:, :], 0.0, ALU.mult, ALU.add),
        inc="V")

    if cut == "p3":
        ctx.close_scope(psc)
        return
    ES = [0] * NT
    pk_war = 0
    for tt in range(4):
        a = 512 * tt
        e2 = p.op("scalar", lambda E, a=a: E.activation(
            packF[0:NH, :], relF[:, a:a + 512], AF.Exp,
            scale=acol[0:NH, 0:1]),
            waits=[("V", rel_v), ("V", av), ("T", pk_war)], inc="S")
        nr = p.op("vector", lambda E, a=a: E.tensor_scalar(
            packF[32:48, :], relF[:, a:a + 512], acol[0:NH, 0:1], None,
            ALU.mult), waits=[("V", rel_v), ("V", av), ("T", pk_war)],
            inc="V")
        if tt < 3:
            g1 = p.op("vector", lambda E, a=a: E.tensor_copy(
                packF[64:80, :], gamFv[:, a + 1:a + 513]),
                waits=[("V", dgF_v), ("V", p.cnt("V"))], inc="V")
        else:
            p.op("vector", lambda E, a=a: E.tensor_copy(
                packF[64:80, 0:511], gamFv[:, a + 1:a + 512]),
                waits=[("V", dgF_v), ("V", p.cnt("V"))], inc="V")
            g1 = p.op("vector", lambda E: E.memset(
                packF[64:80, 511:512], 0.0), inc="V")
        for cc in range(4):
            c = 4 * tt + cc
            p.op("vector", lambda E, c=c, cc=cc, a=a: E.tensor_scalar(
                packT[0:16, 128 * cc:128 * (cc + 1)],
                relF[:, 128 * c:128 * (c + 1)],
                relF[:, 128 * c + 127:128 * c + 128], None, ALU.subtract),
                waits=[("V", rel_v), ("T", pk_war)] if cc == 0 else (),
                inc="V")
        es = p.op("scalar", lambda E: E.activation(
            packF[96:112, :], packT[0:16, :], AF.Exp,
            scale=nacol[0:NH, 0:1]), waits=[("V", p.cnt("V"))], inc="S")
        if cut == "p3a":
            continue
        wv = p.op("vector", lambda E, a=a: E.tensor_copy(
            th64[:, :], phiF[:, a:a + 512]),
            waits=[("V", phi_v), ("S", es)], inc="V")
        wv = p.op("vector", lambda E: E.tensor_scalar(
            sinb[:, :], th64[:, :], 1.0 / TWO_PI, RMAGIC,
            ALU.mult, ALU.add), inc="V")
        wv = p.op("vector", lambda E: E.tensor_scalar(
            sinb[:, :], sinb[:, :], RMAGIC, None, ALU.subtract), inc="V")
        wv = p.op("vector", lambda E: E.scalar_tensor_tensor(
            sinb[:, :], sinb[:, :], -TWO_PI, th64[:, :],
            ALU.mult, ALU.add), inc="V")
        wv = p.op("vector", lambda E: E.tensor_scalar(
            sinb[:, :], sinb[:, :], -math.pi, math.pi, ALU.max, ALU.min),
            inc="V")
        s1 = p.op("scalar", lambda E: E.activation(
            packT[32:64, :], sinb[:, :], AF.Sin), waits=[("V", wv)],
            inc="S")
        wv = p.op("vector", lambda E: E.tensor_scalar(
            rscA[:, :], th64[:, :], 1.0 / TWO_PI, 0.25,
            ALU.mult, ALU.add), inc="V")
        wv = p.op("vector", lambda E: E.tensor_scalar(
            rscA[:, :], rscA[:, :], RMAGIC, None, ALU.add), inc="V")
        wv = p.op("vector", lambda E: E.tensor_scalar(
            rscA[:, :], rscA[:, :], RMAGIC, None, ALU.subtract), inc="V")
        wv = p.op("vector", lambda E: E.tensor_scalar(
            th64[:, :], th64[:, :], math.pi / 2.0, None,
            ALU.add), inc="V")
        wv = p.op("vector", lambda E: E.scalar_tensor_tensor(
            rscA[:, :], rscA[:, :], -TWO_PI, th64[:, :],
            ALU.mult, ALU.add), inc="V")
        wv = p.op("vector", lambda E: E.tensor_scalar(
            rscA[:, :], rscA[:, :], -math.pi, math.pi, ALU.max, ALU.min),
            inc="V")
        s2 = p.op("scalar", lambda E: E.activation(
            packT[0:32, :], rscA[:, :], AF.Sin), waits=[("V", wv)], inc="S")
        if cut == "p3b":
            continue
        for cc in range(4):
            c = 4 * tt + cc
            bk = 2 + (c % 2)
            tv = p.op("tensor", lambda E, cc=cc, bk=bk: E.transpose(
                ctx.pb(bk, 128), packF[:, 128 * cc:128 * (cc + 1)],
                ctx.idf[:]),
                waits=([("V", g1), ("S", e2), ("S", es), ("V", nr)]
                       + ctx.guard(bk)), inc="T")
            tv2 = p.op("tensor", lambda E, cc=cc, bk=bk: E.transpose(
                ctx.pb(bk, 64, off=128), packT[:, 128 * cc:128 * (cc + 1)],
                ctx.idf[0:64, 0:64]),
                waits=[("S", s1), ("S", s2)], inc="T")
            v1 = p.op("vector", lambda E, c=c, bk=bk: E.tensor_copy(
                tmE2[:, c, :], ctx.pb(bk, 16)),
                waits=[("T", tv2)], inc="V")
            p.op("vector", lambda E, c=c, bk=bk: E.tensor_copy(
                tmNR[:, c, :], ctx.pb(bk, 16, off=32)), inc="V")
            p.op("vector", lambda E, c=c, bk=bk: E.tensor_copy(
                tmAt[:, c, 0:16], ctx.pb(bk, 16, off=64)), inc="V")
            p.op("vector", lambda E, c=c, bk=bk: E.tensor_copy(
                tmAt[:, c, 16:32], ctx.pb(bk, 16, off=96)), inc="V")
            v2 = p.op("vector", lambda E, c=c, bk=bk: E.tensor_copy(
                tmB[:, c, :], ctx.pb(bk, 64, off=128)),
                waits=[("T", tv2)], inc="V")
            ctx.rguard(bk, "V", v2)
            ES[c] = v2
            pk_war = tv2
    tmAB_v = p.cnt("V")

    if cut in ("p4", "p3a", "p3b"):
        ctx.close_scope(psc)
        return
    dl_v = de_v = None
    for c in range(NT):
        dl_v = p.op("vector", lambda E, c=c: E.tensor_tensor(
            dlt[:, c, :], alp[:, c, :], tmAt[:, c, 0:16], ALU.add),
            waits=[("V", ES[c])], inc="V")
    for c in range(NT):
        de_v = p.op("vector", lambda E, c=c: E.tensor_tensor(
            dEs[:, c, :], dlt[:, c, :], tmAt[:, c, 16:32], ALU.mult),
            waits=[("V", ES[NT - 1])] if c == 0 else (), inc="V")

    msq = ctx.scoped("msq", [128, NT, 2], F32)
    rsq = ctx.scoped("rsq", [128, NT, 2], F32)
    vv = None
    for g in range(4):
        for j in range(8):
            c = 4 * g + j // 2
            off = 64 * (j % 2)
            vv = p.op("vector", lambda E, c=c, off=off, j=j: E.tensor_tensor(
                ctx.f32scr[:, 64 * j:64 * (j + 1)],
                small[:, c, off:off + 64], small[:, c, off:off + 64],
                ALU.mult),
                waits=[("S", smv), ("IN", iv_sm)] if (g == 0 and j == 0)
                else (), inc="V")
        vv = p.op("vector", lambda E, g=g: E.tensor_reduce(
            msq[:, 4 * g:4 * (g + 1), :].rearrange("p a b -> p (a b)"),
            ctx.f32scr[:, 0:512].rearrange("p (a b) -> p a b", b=64),
            AX.X, ALU.add), waits=[("V", vv)], inc="V")
    vv = p.op("vector", lambda E: E.tensor_scalar(
        rsq.rearrange("p a b -> p (a b)"),
        msq.rearrange("p a b -> p (a b)"), 1.0 / NS, EPS2,
        ALU.mult, ALU.add), waits=[("V", vv)], inc="V")
    p.op("scalar", lambda E: E.activation(
        rsq.rearrange("p a b -> p (a b)"),
        rsq.rearrange("p a b -> p (a b)"), AF.Ln),
        waits=[("V", vv)], inc="S")
    rs_v = p.op("scalar", lambda E: E.activation(
        rsq.rearrange("p a b -> p (a b)"),
        rsq.rearrange("p a b -> p (a b)"), AF.Exp, scale=-0.5), inc="S")

    bc_v = None
    for c in range(NT):
        for (oi, off, sc_t, bi_t, dst) in ((0, 0, bcB, bBt, bsf),
                                           (1, 64, bcC, bCt, csf)):
            vv = p.op("vector", lambda E, c=c, off=off, oi=oi, sc_t=sc_t:
                      E.scalar_tensor_tensor(
                ctx.f32scr[:, 128:192], small[:, c, off:off + 64],
                rsq[:, c, oi:oi + 1], sc_t[:, :], ALU.mult, ALU.mult),
                waits=[("S", rs_v)] if c == 0 and oi == 0 else (), inc="V")
            vv = p.op("vector", lambda E, c=c, bi_t=bi_t, dst=dst:
                      E.tensor_tensor(
                dst[:, c, :], ctx.f32scr[:, 128:192], bi_t[:, :], ALU.add),
                inc="V")
        for (src, dst) in ((bsf, btm), (csf, ctm)):
            p.op("vector", lambda E, c=c, src=src: E.tensor_tensor(
                ctx.f32scr[:, 0:32], src[:, c, 0:32], tmB[:, c, 0:32],
                ALU.mult), inc="V")
            p.op("vector", lambda E, c=c, src=src: E.tensor_tensor(
                ctx.f32scr[:, 32:64], src[:, c, 32:64], tmB[:, c, 32:64],
                ALU.mult), inc="V")
            p.op("vector", lambda E, c=c, dst=dst: E.tensor_tensor(
                dst[:, c, 0:32], ctx.f32scr[:, 0:32], ctx.f32scr[:, 32:64],
                ALU.subtract), inc="V")
            p.op("vector", lambda E, c=c, src=src: E.tensor_tensor(
                ctx.f32scr[:, 0:32], src[:, c, 0:32], tmB[:, c, 32:64],
                ALU.mult), inc="V")
            p.op("vector", lambda E, c=c, src=src: E.tensor_tensor(
                ctx.f32scr[:, 32:64], src[:, c, 32:64], tmB[:, c, 0:32],
                ALU.mult), inc="V")
            bc_v = p.op("vector", lambda E, c=c, dst=dst: E.tensor_tensor(
                dst[:, c, 32:64], ctx.f32scr[:, 0:32], ctx.f32scr[:, 32:64],
                ALU.add), inc="V")
    for g in range(2):
        for j in range(8):
            c = 8 * g + j
            vv = p.op("vector", lambda E, c=c, j=j: E.tensor_tensor(
                ctx.f32scr[:, 64 * j:64 * (j + 1)], bsf[:, c, :],
                csf[:, c, :], ALU.mult), inc="V")
        vv = p.op("vector", lambda E, g=g: E.tensor_reduce(
            colscr[:, 8 * g:8 * (g + 1)],
            ctx.f32scr[:, 0:512].rearrange("p (a b) -> p a b", b=64),
            AX.X, ALU.add), waits=[("V", vv)], inc="V")
    vv = p.op("vector", lambda E: E.tensor_scalar(
        colscr2[:, :], colscr[:, :], -1.0, None, ALU.mult),
        waits=[("V", vv)], inc="V")
    for c in range(NT):
        cf_v = p.op("vector", lambda E, c=c: E.scalar_tensor_tensor(
            coef[:, c, :], tmAt[:, c, 0:16], colscr2[:, c:c + 1],
            dcol128[:, :], ALU.mult, ALU.add),
            waits=[("V", vv), ("V", ES[c])] if c == 0 else
            [("V", ES[c])], inc="V")
    prep_done = p.fence()
    ctx.close_scope(psc)
    if cut == "prep":
        return

    msc = len(ctx._scopes)
    winb = ctx.scoped("winb", [128, KC, 2 * DI], BF16)
    xsil = ctx.scoped("xsil", [128, GRP, DI], BF16)
    zsil = ctx.scoped("zsil", [128, GRP, DI], BF16)
    gyF = ctx.scoped("gyF", [128, NH, 128], BF16)
    if getattr(ctx, "want_dbg", False):
        ctx.dbg.update(gyF=gyF)
    _dbg_main = getattr(ctx, "want_dbg", False)
    woutb = ctx.scoped("woutb", [128, 2, D], BF16)
    s_f = FM[64:128, :].rearrange("n (h d) -> n h d", d=HD)
    s_bb = ctx.scoped("s_bb", [NS, 1, HD], BF16)
    dec_bc = ctx.f32scr[64:64 + NS, 400:416]
    deccol = ctx.scoped("deccol", [NH, 1], F32)
    decrow = ctx.f32scr[0:1, 384:400]
    bFc = ctx.scoped("bFc", [NS, 128], BF16)
    cFc = ctx.scoped("cFc", [NS, 128], BF16)
    rmsb = ctx.scoped("rmsb", [128, 128], BF16)
    wsb = ctx.scoped("wsb", [128, 1, 128], BF16)
    gsb = ctx.scoped("gsb", [128, 1, 128], BF16)
    ceF = ctx.scoped("ceF", [NS, 1, 128], BF16)
    cetm = ctx.scoped("cetm", [128, 1, NS], BF16)
    bstb = ctx.scoped("bstb", [128, 1, NS], BF16)
    gytm = ctx.scoped("gytm", [128, 1, 128], BF16)
    yscr = ctx.f32scr[:, 0:128]
    if _dbg_main:
        ctx.dbg.update(wsb=wsb, gsb=gsb, rmsb=rmsb, ceF=ceF, s_bb=s_bb,
                       selhf=selhf, bstb=bstb, cetm=cetm)

    wk = []
    for k in range(KC):
        p.op("gpsimd", lambda E, k=k: E.dma_start(
            winb[:, k, :],
            dram["w_in"][128 * k:128 * (k + 1), 0:4096]),
            waits=prep_done if k == 0 else (), inc="WG")
        wk.append(p.cnt("WG") * 16)

    cw_v = None
    for cb in range(8):
        for k in range(KC):
            p.op("tensor", lambda E, k=k, cb=cb: E.matmul(
                ctx.prow(2), gcol[:, k:k + 1],
                winb[:, k, 512 * cb:512 * (cb + 1)],
                start=(k == 0), stop=(k == KC - 1)),
                waits=([("WG", wk[-1])] + ctx.guard(2) + prep_done)
                if (cb == 0 and k == 0) else (ctx.guard(2) if k == 0 else ()))
        tv = None
        for k in range(KC):
            tv = p.op("tensor", lambda E, k=k, cb=cb: E.matmul(
                ctx.prow(3), bcol[:, k:k + 1],
                winb[:, k, 512 * cb:512 * (cb + 1)],
                start=(k == 0), stop=(k == KC - 1)),
                waits=ctx.guard(3) if k == 0 else (),
                inc="T" if k == KC - 1 else None)
        v2 = p.op("vector", lambda E, cb=cb: E.tensor_copy(
            csxz[0:1, 512 * cb:512 * (cb + 1)], ctx.prow(2)),
            waits=[("T", tv)], inc="V")
        ctx.rguard(2, "V", v2)
        cw_v = p.op("vector", lambda E, cb=cb: E.tensor_copy(
            ctx.stage_b[0:1, 2048 + 512 * cb:2048 + 512 * (cb + 1)],
            ctx.prow(3)), inc="V")
        ctx.rguard(3, "V", cw_v)
    p.op("sync", lambda E: E.dma_start(
        bwxz[0:1, :], ctx.stage_b[0:1, 2048:6144]),
        waits=[("V", cw_v)], inc="IN")
    xz_dma = p.cnt("IN") * 16
    gs_v = None
    for k in range(KC):
        gs_v = p.op("vector", lambda E, k=k: E.tensor_scalar(
            winb[:, k, :], winb[:, k, :], gcolf[:, k:k + 1], None,
            ALU.mult),
            waits=[("V", cw_v)] if k == 0 else (), inc="V")
    sinit = p.op("vector", lambda E: E.memset(
        s_f.rearrange("n h d -> n (h d)"), 0.0), waits=prep_done, inc="V")
    mr_v = None
    for a in range(0, T, 512):
        p.op("vector", lambda E, a=a: E.tensor_copy(
            ctx.f32scr[0:1, 0:512], ctx.r_row[0:1, a:a + 512]),
            waits=prep_done if a == 0 else (), inc="V")
        mr_v = p.op("vector", lambda E, a=a: E.tensor_tensor(
            mur_row[0:1, a:a + 512], ctx.mu_row[0:1, a:a + 512],
            ctx.f32scr[0:1, 0:512], ALU.mult), inc="V")
    rp_v = p.op("vector", lambda E: E.tensor_scalar(
        r1p1_row[0:1, :], ctx.r_row[0:1, :], 1.0, None, ALU.add), inc="V")
    row_ov = [("V", rp_v)]

    wout_next = None

    def load_wout(di, tdep):
        buf = di % 2
        p.op("gpsimd", lambda E, di=di, buf=buf: E.dma_start(
            woutb[:, buf, :], dram["w_out"][128 * di:128 * (di + 1), :]),
            waits=tdep, inc="WG")
        return p.cnt("WG") * 16

    silu_done = {}
    xz_war = 0
    for g in range(NT // GRP):
        for ci in range(GRP):
            c = GRP * g + ci
            sv = None
            for k in range(KC):
                sv = p.op("scalar", lambda E, k=k, c=c: E.copy(
                    ctx.xbc[:, k, :], x[:, k, 128 * c:128 * (c + 1)]),
                    waits=[("T", xz_war)] if k == 0 else (), inc="S")
            for cb in range(8):
                bk = cb % 2
                tv = None
                for k in range(KC):
                    w = []
                    if k == 0:
                        w = ctx.guard(bk) + [("WG", wk[KC - 1]),
                                             ("V", gs_v), ("S", sv),
                                             ("IN", xz_dma),
                                             ("IN", ctx.ir12_dma)]
                    tv = p.op("tensor", lambda E, k=k, c=c, cb=cb, bk=bk:
                              E.matmul(
                        ctx.pb(bk),
                        ctx.xbc[:, k, :],
                        winb[:, k, 512 * cb:512 * (cb + 1)],
                        start=(k == 0), stop=False), waits=w)
                tv = p.op("tensor", lambda E, c=c, cb=cb, bk=bk: E.matmul(
                    ctx.pb(bk), ctx.mu2[:, 128 * c:128 * (c + 1)],
                    csbw[:, 512 * cb:512 * (cb + 1)],
                    start=False, stop=True), inc="T")
                xz_war = tv
                dst = xsil if cb < 4 else zsil
                o = (cb % 4) * 512
                sv2 = p.op("scalar", lambda E, c=c, ci=ci, cb=cb, bk=bk,
                           dst=dst, o=o: E.activation(
                    dst[:, ci, o:o + 512], ctx.pb(bk), AF.Silu,
                    scale=lncols[:, c, 0:1]), waits=[("T", tv)], inc="S")
                ctx.rguard(bk, "S", sv2)
                silu_done[(g, ci)] = sv2

        for ci in range(GRP if cut != "inproj" else 0):
            c = GRP * g + ci
            tb = p.op("tensor", lambda E, c=c: E.transpose(
                ctx.pt(6, 128, off=0)[0:NS, :], btm[:, c, :], ctx.idb[:]),
                waits=ctx.guard(6) + prep_done, inc="T")
            tc = p.op("tensor", lambda E, c=c: E.transpose(
                ctx.pt(6, 128, off=128)[0:NS, :], ctm[:, c, :], ctx.idb[:]),
                inc="T")
            vb = p.op("vector", lambda E: E.tensor_copy(
                bFc[:, :], ctx.pt(6, 128, off=0)[0:NS, :]),
                waits=[("T", tc)], inc="V")
            vc = p.op("vector", lambda E: E.tensor_copy(
                cFc[:, :], ctx.pt(6, 128, off=128)[0:NS, :]), inc="V")
            ctx.rguard(6, "V", vc)
            tv = p.op("tensor", lambda E: E.matmul(
                ctx.pb(0, 128), bFc[0:NS, :], cFc[0:NS, :],
                start=True, stop=True),
                waits=[("V", vc)] + ctx.guard(0), inc="T")
            rv = p.op("vector", lambda E: E.tensor_tensor(
                rmsb[:, :], ctx.pb(0, 128), ctx.tri[:, :], ALU.mult),
                waits=[("T", tv)], inc="V")
            ctx.rguard(0, "V", rv)
            dv = p.op("vector", lambda E, c=c: E.tensor_scalar(
                deccol[0:NH, 0:1], relF[:, 128 * c + 127:128 * c + 128],
                acol[0:NH, 0:1], None, ALU.mult),
                waits=row_ov, inc="V")
            sv = p.op("scalar", lambda E: E.activation(
                deccol[0:NH, 0:1], deccol[0:NH, 0:1], AF.Exp),
                waits=[("V", dv)], inc="S")
            tv = p.op("tensor", lambda E: E.transpose(
                ctx.pb(4, NH)[0:1, :], deccol[0:NH, 0:1],
                ctx.idf[0:NH, 0:NH]),
                waits=[("S", sv)] + ctx.guard(4), inc="T")
            vv = p.op("vector", lambda E: E.tensor_copy(
                decrow[0:1, :], ctx.pb(4, NH)[0:1, :]), waits=[("T", tv)],
                inc="V")
            tv = p.op("tensor", lambda E: E.matmul(
                ctx.pb(4, NH, off=128)[0:NS, :], ctx.onerf_all[0:1, 0:NS],
                decrow[0:1, :], start=True, stop=True),
                waits=[("V", vv)], inc="T")
            dbv = p.op("vector", lambda E: E.tensor_copy(
                dec_bc[:, :], ctx.pb(4, NH, off=128)[0:NS, :]),
                waits=[("T", tv)], inc="V")
            ctx.rguard(4, "V", dbv)

            tslot_guard = [0, 0, 0, 0]
            last_gv = 0
            last_cet = 0
            last_tvy = 0
            last_tvs = 0
            last_gyt = 0
            for h in range(NH):
                pg = h % 2
                bkD = 2 + pg
                bkY = 0 + pg
                bkS = 4 + pg
                tv = p.op("tensor", lambda E, h=h, c=c, bkD=bkD: E.matmul(
                    ctx.pb(bkD, 128), selhf[:, 128 * h:128 * (h + 1)],
                    relF[:, 128 * c:128 * (c + 1)],
                    start=True, stop=True),
                    waits=ctx.guard(bkD) + ([("V", rel_v), ("V", selsc)]
                                            if h == 0 else []), inc="T")
                sv = p.op("scalar", lambda E, h=h, c=c, bkD=bkD:
                          E.activation(
                    ctx.f32scr[:, 256:384], ctx.pb(bkD, 128), AF.Relu,
                    bias=tmNR[:, c, h:h + 1]),
                    waits=[("T", tv), ("V", last_gv)], inc="S")
                ctx.rguard(bkD, "S", sv)
                sv = p.op("scalar", lambda E: E.activation(
                    wsb[:, 0, :], ctx.f32scr[:, 256:384], AF.Exp,
                    scale=-1.0), inc="S")
                gv_h = p.op("vector", lambda E, h=h, c=c:
                            E.scalar_tensor_tensor(
                    gsb[:, 0, :], wsb[:, 0, :], dlt[:, c, h:h + 1],
                    rmsb[:, :], ALU.mult, ALU.mult),
                    waits=[("S", sv), ("V", rv), ("T", last_tvy)], inc="V")
                last_gv = gv_h
                sl = (2 * h) % 4
                cv = p.op("vector", lambda E, h=h, c=c: E.tensor_scalar(
                    cetm[:, 0, :], ctm[:, c, :],
                    tmE2[:, c, h:h + 1], None, ALU.mult),
                    waits=[("T", last_cet)], inc="V")
                ce_t = p.op("tensor", lambda E, sl=sl: E.transpose(
                    ctx.pt(6 + sl // 2, 128, off=128 * (sl % 2))[0:NS, :],
                    cetm[:, 0, :], ctx.idb[:]),
                    waits=[("V", cv), ("V", tslot_guard[sl])], inc="T")
                last_cet = ce_t
                cev = p.op("vector", lambda E, sl=sl: E.tensor_copy(
                    ceF[:, 0, :],
                    ctx.pt(6 + sl // 2, 128, off=128 * (sl % 2))[0:NS, :]),
                    waits=[("T", ce_t), ("T", last_tvy)], inc="V")
                tslot_guard[sl] = cev
                bst_v = p.op("vector", lambda E, h=h, c=c: E.tensor_scalar(
                    bstb[:, 0, :], btm[:, c, :], dEs[:, c, h:h + 1],
                    None, ALU.mult), waits=[("T", last_tvs)], inc="V")
                sbv = p.op("vector", lambda E, h=h: E.tensor_copy(
                    s_bb[:, 0, :], s_f[:, h, :]),
                    waits=[("V", sinit), ("T", last_tvy)], inc="V")
                w = ctx.guard(bkY) + [("V", cev), ("V", sbv),
                                      ("S", silu_done[(g, ci)])]
                p.op("tensor", lambda E, bkY=bkY: E.matmul(
                    ctx.pb(bkY, 128), ceF[:, 0, :], s_bb[:, 0, :],
                    start=True, stop=False), waits=w)
                tvy = p.op("tensor", lambda E, h=h, ci=ci, bkY=bkY:
                           E.matmul(
                    ctx.pb(bkY, 128), gsb[:, 0, :],
                    xsil[:, ci, 128 * h:128 * (h + 1)],
                    start=False, stop=True),
                    waits=[("V", gv_h)], inc="T")
                last_tvy = tvy
                tvs = p.op("tensor", lambda E, h=h, ci=ci, bkS=bkS:
                           E.matmul(
                    ctx.pb(bkS, 128)[0:NS, :], bstb[:, 0, :],
                    xsil[:, ci, 128 * h:128 * (h + 1)],
                    start=True, stop=True),
                    waits=ctx.guard(bkS) + [("V", bst_v)], inc="T")
                last_tvs = tvs
                vv = p.op("vector", lambda E, h=h, c=c, ci=ci, bkY=bkY:
                          E.scalar_tensor_tensor(
                    yscr[:, :], xsil[:, ci, 128 * h:128 * (h + 1)],
                    coef[:, c, h:h + 1], ctx.pb(bkY, 128),
                    ALU.mult, ALU.add), waits=[("T", tvy)], inc="V")
                ctx.rguard(bkY, "V", vv)
                sv_up = p.op("vector", lambda E, h=h, bkS=bkS:
                             E.scalar_tensor_tensor(
                    s_f[:, h, :], s_f[:, h, :], dec_bc[:, h:h + 1],
                    ctx.pb(bkS, 128)[0:NS, :], ALU.mult, ALU.add),
                    waits=[("T", tvs), ("V", dbv)], inc="V")
                ctx.rguard(bkS, "V", sv_up)
                gy_v = p.op("vector", lambda E, h=h, ci=ci: E.tensor_tensor(
                    gytm[:, 0, :], yscr[:, :],
                    zsil[:, ci, 128 * h:128 * (h + 1)], ALU.mult),
                    waits=[("T", last_gyt)], inc="V")
                sl = (2 * h + 1) % 4
                gy_t = p.op("tensor", lambda E, sl=sl: E.transpose(
                    ctx.pt(6 + sl // 2, 128, off=128 * (sl % 2)),
                    gytm[:, 0, :], ctx.idb[:]),
                    waits=[("V", gy_v), ("V", tslot_guard[sl])], inc="T")
                last_gyt = gy_t
                gyv = p.op("vector", lambda E, h=h, ci=ci, sl=sl:
                           E.tensor_copy(
                    gyF[:, h, 128 * ci:128 * (ci + 1)],
                    ctx.pt(6 + sl // 2, 128, off=128 * (sl % 2))),
                    waits=[("T", gy_t)], inc="V")
                tslot_guard[sl] = gyv
        ctx.rguard(6, "V", p.cnt("V"))
        ctx.rguard(7, "V", p.cnt("V"))
        gy_done = p.cnt("V")
        if cut in ("inproj", "scan"):
            continue

        c = g
        if wout_next is None:
            wout_next = load_wout(0, [("T", p.cnt("T"))])
        po_done = {}
        t_marks = {}
        for di in range(NH):
            w = [("WG", wout_next)]
            if di == 0:
                w += [("V", gy_done)] + ctx.guard(0, 1)
            for do in range(KC):
                bk = do // 4
                off = (do % 4) * 128
                po_done[do] = p.op("tensor", lambda E, di=di, do=do, bk=bk,
                                   off=off: E.matmul(
                    ctx.pb(bk, 128, off=off),
                    woutb[:, di % 2, 128 * do:128 * (do + 1)],
                    gyF[:, di, :],
                    start=(di == 0 and do % 4 == 0), stop=(di == NH - 1),
                    skip_group_check=True),
                    waits=w if do == 0 else (),
                    inc="T" if do == KC - 1 else None)
                w = ()
            t_marks[di] = p.cnt("T")
            if di < NH - 1:
                wout_next = load_wout(
                    di + 1, [("T", t_marks[di - 1])] if di >= 1 else ())
        tv = p.op("tensor", lambda E, c=c: E.matmul(
            ctx.pb(4, 128), ctx.onerf_all[64:65, :],
            r1p1_row[0:1, 128 * c:128 * (c + 1)], start=True, stop=True),
            waits=ctx.guard(4), inc="T")
        tv = p.op("tensor", lambda E, c=c: E.matmul(
            ctx.pb(5, 128), ctx.onerf_all[0:1, :],
            mur_row[0:1, 128 * c:128 * (c + 1)], start=True, stop=True),
            waits=ctx.guard(5), inc="T")
        for do in range(KC):
            bk = do // 4
            off = (do % 4) * 128
            a = 128 * c
            p.op("vector", lambda E, do=do, a=a: E.tensor_tensor(
                ctx.f32scr[:, 128:256], x[:, do, a:a + 128],
                ctx.pb(4, 128), ALU.mult),
                waits=[("T", tv), ("T", po_done[do])], inc="V")
            p.op("vector", lambda E: E.tensor_tensor(
                ctx.f32scr[:, 128:256], ctx.f32scr[:, 128:256],
                ctx.pb(5, 128), ALU.subtract), inc="V")
            vv = p.op("vector", lambda E, do=do, a=a, bk=bk, off=off:
                      E.tensor_tensor(
                x[:, do, a:a + 128], ctx.f32scr[:, 128:256],
                ctx.pb(bk, 128, off=off), ALU.add), inc="V")
            ctx.rguard(bk, "V", vv)
        ctx.rguard(4, "V", p.cnt("V"))
        ctx.rguard(5, "V", p.cnt("V"))
        wout_next = load_wout(0, [("T", p.cnt("T"))]) if g < NT - 1 \
            else None
    ctx.close_scope(msc)



_NC_CACHE = {}


def _get_nc(phases):
    key = tuple(phases)
    if key not in _NC_CACHE:
        _NC_CACHE[key] = build_nc(phases)
    return _NC_CACHE[key]


def _sel(n):
    m = np.zeros((n, n * 128), np.float32)
    for i in range(n):
        m[i, i * 128:(i + 1) * 128] = 1.0
    return m


def make_inmaps(inputs):
    f = np.float32
    u = np.ascontiguousarray(inputs["L0_attn_u"], dtype=f)
    rep = lambda v, n: np.ascontiguousarray(
        np.broadcast_to(np.asarray(v, f).reshape(1, -1), (128, n)))
    shared = dict(
        w_in=np.ascontiguousarray(inputs["L0_attn_in_proj_w"], f),
        w_theta=np.ascontiguousarray(inputs["L0_attn_theta_w"], f),
        w_out=np.ascontiguousarray(inputs["L0_attn_out_proj_w"], f),
        wg=np.ascontiguousarray(inputs["L0_ffn_Wg"], f),
        bg_col=np.ascontiguousarray(inputs["L0_ffn_bg"], f).reshape(NE, 1),
        w1=np.ascontiguousarray(inputs["L0_ffn_W1"], f),
        w2=np.ascontiguousarray(inputs["L0_ffn_W2"], f),
        b1T=np.ascontiguousarray(np.transpose(inputs["L0_ffn_b1"]), f),
        b2=np.ascontiguousarray(inputs["L0_ffn_b2"], f),
        gam=np.ascontiguousarray(
            np.asarray(inputs["L0_attn_norm_scale"], f).reshape(KC, 128).T),
        bet=np.ascontiguousarray(
            np.asarray(inputs["L0_attn_norm_bias"], f).reshape(KC, 128).T),
        a_log_col=np.ascontiguousarray(
            inputs["L0_attn_A_log"], f).reshape(NH, 1),
        dt_bias128=rep(inputs["L0_attn_dt_bias"], NH),
        d128=rep(inputs["L0_attn_D"], NH),
        bcB128=rep(inputs["L0_attn_bcB_scale"], NS),
        bcC128=rep(inputs["L0_attn_bcC_scale"], NS),
        bB128=rep(inputs["L0_attn_B_bias"], NS),
        bC128=rep(inputs["L0_attn_C_bias"], NS),
        selh=_sel(NH),
        sel8=_sel(NE),
        ident_f=np.eye(128, dtype=f),
        ident_b2=np.eye(128, dtype=f),
        tri=np.ascontiguousarray(np.tri(128, 128, 0, dtype=f).T),
        ones_col=np.ones((128, 1), f),
        ones_row=np.ones((1, 128), f),
        ones65=np.ones((65, 128), f),
    )
    in_maps = []
    for b in range(NCORES):
        m = dict(shared)
        m["xT"] = np.ascontiguousarray(u[b].T)
        in_maps.append(m)
    return in_maps


def kernel(**inputs):
    nc = _get_nc(("ffn1", "attn", "ffn2"))
    in_maps = make_inmaps(inputs)
    res = run_bass_kernel_spmd(nc, in_maps, core_ids=list(range(NCORES)))
    out = np.stack([np.ascontiguousarray(r["out"].T) for r in res.results])
    return out.astype(np.float32)


# version 04
# speedup vs baseline: 1.0464x; 1.0464x over previous
import math
import numpy as np
import ml_dtypes
import concourse.bass as bass
import concourse.mybir as mybir
from concourse.bass_utils import run_bass_kernel_spmd

F32 = mybir.dt.float32
BF16 = mybir.dt.bfloat16
I32 = mybir.dt.int32
AF = mybir.ActivationFunctionType
ALU = mybir.AluOpType
AX = mybir.AxisListType

D = 1024
T = 2048
KC = D // 128
NT = T // 128
DI = 2048
NH = 16
HD = 128
NS = 64
NE = 8
PROJ = 4256
EPS1 = 1e-5
EPS2 = 1e-6
NCORES = 8
GRP = 1
TWO_PI = 2.0 * math.pi
RMAGIC = 12582912.0


class Prog:
    ENGINES = ("tensor", "vector", "scalar", "gpsimd", "sync")

    def __init__(self, nc):
        self.nc = nc
        self.plan = {e: [] for e in self.ENGINES}
        self.sems = {}
        self.counts = {}

    def sem(self, name):
        self.sems[name] = self.nc.alloc_semaphore(name=name)
        self.counts[name] = 0

    DMA_SEMS = ("WG", "IN", "OUT")

    def op(self, eng, fn, waits=(), inc=None, amt=1):
        newv = None
        if inc is not None:
            self.counts[inc] += amt
            newv = self.counts[inc]
        wl = []
        seen = {}
        for (s, v) in waits:
            if v and v > seen.get(s, 0):
                seen[s] = v
        wl = [(self.sems[s], v) for s, v in seen.items()]
        isem = self.sems[inc] if inc is not None else None
        ia = amt * 16 if inc in self.DMA_SEMS else amt

        def closure(E, wl=wl, fn=fn, isem=isem, ia=ia):
            for h, v in wl:
                E.wait_ge(h, v)
            ins = fn(E)
            if isem is not None:
                ins.then_inc(isem, ia)
        self.plan[eng].append(closure)
        return newv

    def cnt(self, name):
        return self.counts[name]

    def fence(self):
        return [("T", self.cnt("T")), ("V", self.cnt("V")),
                ("S", self.cnt("S")), ("G", self.cnt("G"))]

    def emit(self):
        with self.nc.Block() as block:
            for eng in self.ENGINES:
                ops = self.plan[eng]
                getattr(block, eng)(lambda E, ops=ops: [f(E) for f in ops])


class Ctx:
    def __init__(self, nc, p, dram):
        self.nc = nc
        self.p = p
        self.dram = dram
        self._scopes = []
        self.dbg = {}
        self.bguard = {}

    def sb(self, name, shape, dtype):
        return self.nc.alloc_sbuf_tensor(name, list(shape), dtype).ap()

    def scoped(self, name, shape, dtype):
        self._uid = getattr(self, "_uid", 0) + 1
        g = self.nc.sbuf_tensor(f"{name}_{self._uid}", list(shape), dtype)
        h = g.__enter__()
        self._scopes.append(g)
        return h.ap()

    def close_scope(self, upto=0):
        while len(self._scopes) > upto:
            g = self._scopes.pop()
            g.__exit__(None, None, None)

    def pb(self, bank, cols=512, off=0):
        assert 0 <= bank < 6 and 512 * bank + off + cols <= 3072
        return self.pA[:, 512 * bank + off: 512 * bank + off + cols]

    def prow(self, bank, cols=512, off=0):
        assert 0 <= bank < 6 and off + cols <= 512
        return self.pA[0:1, 512 * bank + off: 512 * bank + off + cols]

    def pt(self, bank, cols=1024, off=0):
        assert bank in (6, 7) and off + cols <= 1024
        b = bank - 6
        return self.pT[:, 1024 * b + off: 1024 * b + off + cols]

    def guard(self, *banks):
        return [self.bguard[b] for b in banks if b in self.bguard]

    def rguard(self, bank, sem, cnt):
        self.bguard[bank] = (sem, cnt)

    def setup(self):
        p, dram = self.p, self.dram
        self.x = self.sb("x", [128, KC, T], F32)
        self.xbc = self.sb("xbc", [128, KC, 128], BF16)
        self.f32scr = self.sb("f32scr", [128, 512], F32)
        self.RF1 = self.sb("RF1", [128, T], F32)
        self.RB = self.sb("RB", [128, 6144], BF16)
        self.mu_row = self.RF1[0:1, :]
        self.r_row = self.RF1[32:33, :]
        self.tmp_row = self.RF1[64:65, :]
        self.r12_row = self.RF1[96:97, :]
        self.r1p1_row = self.RF1[64:65, :]
        self.mur_row = self.RF1[0:1, :]
        self.mu_b = self.RB[32:33, 0:T]
        self.ir12_b = self.RB[33:34, 0:T]
        self.mu2 = self.RB[32:34, 0:T]
        self.stage_b = self.RB[0:1, :]
        self.idf = self.sb("idf_s", [128, 128], F32)
        self.idb = self.sb("idb", [128, 128], BF16)
        self.tri = self.sb("tri_s", [128, 128], BF16)
        self.onec = self.sb("onec", [128, 1], BF16)
        self.onecf = self.sb("onecf", [128, 1], F32)
        self.onerf = self.sb("onerf", [1, 128], F32)
        self.onerf_all = self.sb("onerf_all", [65, 128], F32)
        self.pA = self.nc.alloc_psum_tensor("pA", [128, 3072], F32).ap()
        self.pT = self.nc.alloc_psum_tensor("pT", [128, 2048], BF16).ap()

        p.op("sync", lambda E: E.dma_start(
            self.x[:], dram["xT"].rearrange("(k p) t -> p k t", p=128)),
            inc="IN")
        self.iv_x = p.cnt("IN") * 16
        for nm, dst in (("ident_f", self.idf),
                        ("ones_col", self.onecf), ("ones_row", self.onerf),
                        ("ones65", self.onerf_all)):
            p.op("sync", lambda E, nm=nm, dst=dst: E.dma_start(
                dst[:], dram[nm]), inc="IN")
        for nm, dst in (("ident_b2", self.idb), ("ones_col", self.onec),
                        ("tri", self.tri)):
            p.op("gpsimd", lambda E, nm=nm, dst=dst: E.dma_start(
                dst[:], dram[nm]), inc="WG")
        self.iv_const = p.cnt("IN") * 16
        self.gv_const = p.cnt("WG") * 16

    def store_out(self):
        p = self.p
        p.op("sync", lambda E: E.dma_start(
            self.dram["out"].rearrange("(k p) t -> p k t", p=128), self.x[:]),
            waits=p.fence(), inc="OUT")
        for nm, ap in self.dbg.items():
            bf = (ap.tensor.dtype == BF16)
            d = self.nc.dram_tensor(
                "dbg_" + nm, list(ap.shape), F32,
                kind="ExternalOutput").ap()
            eng = "gpsimd" if bf else "sync"
            sem = "WG" if bf else "OUT"
            p.op(eng, lambda E, d=d, ap=ap: E.dma_start(d, ap),
                 waits=p.fence(), inc=sem)
        ov = p.cnt("OUT") * 16
        gv = p.cnt("WG") * 16
        p.op("sync", lambda E: E.nop(), waits=[("OUT", ov), ("WG", gv)])


def build_nc(phases=("ffn1", "attn", "ffn2"), dbg=False):
    nc = bass.Bass("TRN2", target_bir_lowering=False,
                   detect_race_conditions=False)
    dram = {}

    def din(name, shape):
        dram[name] = nc.dram_tensor(name, list(shape), F32,
                                    kind="ExternalInput").ap()

    din("xT", [D, T])
    din("w_in", [D, PROJ])
    din("w_theta", [D, 32])
    din("w_out", [DI, D])
    din("wg", [D, NE])
    din("bg_col", [NE, 1])
    din("w1", [NE, D, D])
    din("w2", [NE, D, D])
    din("b1T", [D, NE])
    din("b2", [NE, D])
    din("gam", [128, KC])
    din("bet", [128, KC])
    din("a_log_col", [NH, 1])
    din("dt_bias128", [128, NH])
    din("d128", [128, NH])
    din("bcB128", [128, NS])
    din("bcC128", [128, NS])
    din("bB128", [128, NS])
    din("bC128", [128, NS])
    din("selh", [NH, NH * 128])
    din("sel8", [NE, NE * 128])
    din("ident_f", [128, 128])
    din("ident_b2", [128, 128])
    din("tri", [128, 128])
    din("ones_col", [128, 1])
    din("ones_row", [1, 128])
    din("ones65", [65, 128])
    dram["out"] = nc.dram_tensor("out", [D, T], F32,
                                 kind="ExternalOutput").ap()
    dram["w_out_bf"] = nc.dram_tensor("w_out_bf", [2 * D, D], BF16,
                                      kind="Internal").ap()

    p = Prog(nc)
    for s in ("WG", "IN", "OUT", "T", "V", "S", "G"):
        p.sem(s)
    ctx = Ctx(nc, p, dram)
    ctx.want_dbg = dbg
    ctx.setup()
    if "ffn1" in phases:
        emit_ffn(ctx)
    if "attn" in phases:
        emit_attn(ctx)
    for cname in ("prep", "inproj", "scan", "p1", "p2", "p3", "p4",
                  "p3a", "p3b", "p2b"):
        if ("attn_" + cname) in phases:
            emit_attn(ctx, cut=cname)
    if "ffn2" in phases:
        emit_ffn(ctx)
    ctx.store_out()
    p.emit()
    return nc



def emit_ln_stats(ctx, attn=False):
    p, x = ctx.p, ctx.x
    scr = ctx.lnscr
    fence = p.fence()
    tq = 0
    for tt in range(8):
        sl = slice(256 * tt, 256 * (tt + 1))
        sv = None
        for k in range(KC):
            sv = p.op("scalar", lambda E, k=k, sl=sl: E.copy(
                scr[:, k, :], x[:, k, sl]),
                waits=(fence + [("IN", ctx.iv_x)]) if (tt == 0 and k == 0)
                else ([("T", tq)] if k == 0 else ()), inc="S")
        tv = None
        for k in range(KC):
            tv = p.op("tensor", lambda E, k=k: E.matmul(
                ctx.prow(4, 256), ctx.onec[:, 0:1], scr[:, k, :],
                start=(k == 0), stop=(k == KC - 1)),
                waits=([("S", sv), ("WG", ctx.gv_const)] + ctx.guard(4)
                       if k == 0 else ()),
                inc="T" if k == KC - 1 else None)
        sv = None
        for k in range(KC):
            sv = p.op("scalar", lambda E, k=k, sl=sl: E.activation(
                scr[:, k, :], x[:, k, sl], AF.Square),
                waits=[("T", tv)] if k == 0 else (), inc="S")
        for k in range(KC):
            tq = p.op("tensor", lambda E, k=k: E.matmul(
                ctx.prow(5, 256), ctx.onec[:, 0:1], scr[:, k, :],
                start=(k == 0), stop=(k == KC - 1)),
                waits=([("S", sv)] + ctx.guard(5)) if k == 0 else (),
                inc="T" if k == KC - 1 else None)
        a = 256 * tt
        v4 = p.op("vector", lambda E, a=a: E.tensor_scalar(
            ctx.mu_row[0:1, a:a + 256], ctx.prow(4, 256), 1.0 / D, None,
            ALU.mult), waits=[("T", tv)], inc="V")
        v5 = p.op("vector", lambda E, a=a: E.tensor_scalar(
            ctx.tmp_row[0:1, a:a + 256], ctx.prow(5, 256), 1.0 / D, None,
            ALU.mult), waits=[("T", tq)], inc="V")
        ctx.rguard(4, "V", v4)
        ctx.rguard(5, "V", v5)
    scr64 = ctx.f32scr[64:65, 0:512]
    for a in range(0, T, 512):
        p.op("vector", lambda E, a=a: E.tensor_tensor(
            scr64, ctx.mu_row[0:1, a:a + 512], ctx.mu_row[0:1, a:a + 512],
            ALU.mult), inc="V")
        p.op("vector", lambda E, a=a: E.tensor_tensor(
            ctx.tmp_row[0:1, a:a + 512], ctx.tmp_row[0:1, a:a + 512],
            scr64, ALU.subtract), inc="V")
    vv = p.op("vector", lambda E: E.tensor_scalar(
        ctx.r12_row[0:1, :], ctx.tmp_row[0:1, :], EPS1, None, ALU.add),
        inc="V")
    p.op("scalar", lambda E: E.activation(
        ctx.r_row[0:1, :], ctx.r12_row[0:1, :], AF.Ln),
        waits=[("V", vv)], inc="S")
    sv = p.op("scalar", lambda E: E.activation(
        ctx.r_row[0:1, :], ctx.r_row[0:1, :], AF.Exp, scale=-0.5), inc="S")
    ctx.ln_done = [("S", sv)]
    if not attn:
        return
    for a in range(0, T, 512):
        p.op("vector", lambda E, a=a: E.tensor_tensor(
            scr64, ctx.r_row[0:1, a:a + 512], ctx.r_row[0:1, a:a + 512],
            ALU.mult), waits=[("S", sv)] if a == 0 else (), inc="V")
        vv = p.op("vector", lambda E, a=a: E.tensor_tensor(
            ctx.r12_row[0:1, a:a + 512], ctx.tmp_row[0:1, a:a + 512],
            scr64, ALU.mult), inc="V")
    vv = p.op("vector", lambda E: E.tensor_scalar(
        ctx.r12_row[0:1, :], ctx.r12_row[0:1, :], EPS2, None, ALU.add),
        inc="V")
    p.op("scalar", lambda E: E.activation(
        ctx.r12_row[0:1, :], ctx.r12_row[0:1, :], AF.Ln),
        waits=[("V", vv)], inc="S")
    sv = p.op("scalar", lambda E: E.activation(
        ctx.r12_row[0:1, :], ctx.r12_row[0:1, :], AF.Exp, scale=-0.5),
        inc="S")
    scr96 = ctx.f32scr[96:97, 0:512]
    for a in range(0, T, 512):
        p.op("vector", lambda E, a=a: E.tensor_copy(
            scr96, ctx.r_row[0:1, a:a + 512]),
            waits=[("S", sv)] if a == 0 else (), inc="V")
        vv = p.op("vector", lambda E, a=a: E.tensor_tensor(
            ctx.r12_row[0:1, a:a + 512], ctx.r12_row[0:1, a:a + 512],
            scr96, ALU.mult), inc="V")
    p.op("vector", lambda E: E.tensor_scalar(
        ctx.mu_b[0:1, :], ctx.mu_row[0:1, :], -1.0, None, ALU.mult), inc="V")
    p.op("vector", lambda E: E.reciprocal(
        ctx.tmp_row[0:1, :], ctx.r12_row[0:1, :]), waits=[("V", vv)],
        inc="V")
    vv = p.op("vector", lambda E: E.tensor_copy(
        ctx.stage_b[0:1, 0:T], ctx.tmp_row[0:1, :]), inc="V")
    p.op("sync", lambda E: E.dma_start(
        ctx.ir12_b[0:1, :], ctx.stage_b[0:1, 0:T]),
        waits=[("V", vv)], inc="IN")
    ctx.ir12_dma = p.cnt("IN") * 16
    ctx.ln_done = [("V", vv), ("IN", ctx.ir12_dma)]



def emit_ffn(ctx):
    p, dram, x = ctx.p, ctx.dram, ctx.x
    bsc = len(ctx._scopes)
    ctx.lnscr = ctx.RB[:, 0:2048].rearrange("p (k c) -> p k c", c=256)
    t1 = ctx.scoped("t1", [128, KC, 1024], BF16)
    hsb = ctx.scoped("hsb", [128, KC, 1024], BF16)
    yac = ctx.scoped("yac", [128, KC, 1024], BF16)
    w1b = ctx.scoped("w1b", [128, KC, D], BF16)
    w2b = ctx.scoped("w2b", [128, KC, D], BF16)
    wgb = ctx.scoped("wgb", [128, KC, NE], BF16)
    b1s = ctx.scoped("b1s", [128, KC, NE], F32)
    b2b = ctx.scoped("b2b", [NE, D], BF16)
    bgc = ctx.scoped("bgc", [NE, 1], F32)
    sel8b = ctx.scoped("sel8b", [NE, NE * 128], BF16)
    logF = ctx.scoped("logF", [NE, 1024], F32)
    ltm = ctx.scoped("ltm", [128, 8, NE], F32)
    lmax = ctx.scoped("lmax", [128, 8], F32)
    mtm = ctx.scoped("mtm", [128, 8, NE], BF16)
    mF = ctx.scoped("mF", [NE, 1024], BF16)
    mbc = ctx.RB[:, 2048:3072]

    fence_ffn = ctx.p.fence()
    emit_ln_stats(ctx, attn=False)
    lnw = ctx.ln_done

    p.op("gpsimd", lambda E: E.dma_start(
        wgb[:], dram["wg"].rearrange("(k p) e -> p k e", p=128)),
        waits=fence_ffn, inc="WG")
    p.op("gpsimd", lambda E: E.dma_start(b2b[:], dram["b2"]), inc="WG")
    p.op("gpsimd", lambda E: E.dma_start(sel8b[:], dram["sel8"]), inc="WG")
    gv = p.cnt("WG") * 16
    p.op("sync", lambda E: E.dma_start(
        b1s[:], dram["b1T"].rearrange("(k p) e -> p k e", p=128)),
        waits=fence_ffn, inc="IN")
    p.op("sync", lambda E: E.dma_start(bgc[:], dram["bg_col"]), inc="IN")
    iv = p.cnt("IN") * 16

    def load_w(wt, dst, e, tdep):
        for k in range(KC):
            p.op("gpsimd", lambda E, k=k, e=e, wt=wt, dst=dst: E.dma_start(
                dst[:, k, :], dram[wt][e, 128 * k:128 * (k + 1), :]),
                waits=tdep if k == 0 else (), inc="WG")
        return p.cnt("WG") * 16

    w1v = load_w("w1", w1b, 0, ())
    w2v = load_w("w2", w2b, 0, ())

    for th in range(2):
        for tts in range(2):
            a = 1024 * th + 512 * tts
            p.op("tensor", lambda E, a=a: E.matmul(
                ctx.pb(4), ctx.onerf_all[0:1, :],
                ctx.mu_row[0:1, a:a + 512],
                start=True, stop=True),
                waits=(lnw + ctx.guard(4, 5)), inc="T")
            tv = p.op("tensor", lambda E, a=a: E.matmul(
                ctx.pb(5), ctx.onerf_all[32:33, :],
                ctx.r_row[0:1, a:a + 512],
                start=True, stop=True), inc="T")
            vv = None
            for k in range(KC):
                p.op("vector", lambda E, k=k, a=a: E.tensor_tensor(
                    ctx.f32scr[:, 0:512], x[:, k, a:a + 512], ctx.pb(4),
                    ALU.subtract), waits=[("T", tv)] if k == 0 else (),
                    inc="V")
                vv = p.op("vector", lambda E, k=k, tts=tts: E.tensor_tensor(
                    t1[:, k, 512 * tts:512 * (tts + 1)],
                    ctx.f32scr[:, 0:512], ctx.pb(5), ALU.mult), inc="V")
            ctx.rguard(4, "V", vv)
            ctx.rguard(5, "V", vv)
        t1_ready = p.cnt("V")

        for tts in range(2):
            tv = None
            for k in range(KC):
                tv = p.op("tensor", lambda E, k=k, tts=tts: E.matmul(
                    ctx.pA[0:NE, 512 * 4:512 * 4 + 512], wgb[:, k, :],
                    t1[:, k, 512 * tts:512 * (tts + 1)],
                    start=(k == 0), stop=(k == KC - 1)),
                    waits=([("V", t1_ready), ("WG", gv), ("IN", iv)]
                           + ctx.guard(4) if k == 0 else ()),
                    inc="T" if k == KC - 1 else None)
            vv = p.op("vector", lambda E, tts=tts: E.tensor_scalar(
                logF[:, 512 * tts:512 * (tts + 1)],
                ctx.pA[0:NE, 512 * 4:512 * 4 + 512], bgc[0:NE, 0:1], None,
                ALU.add), waits=[("T", tv)], inc="V")
            ctx.rguard(4, "V", vv)
        logv = p.cnt("V")
        tv = None
        for c8 in range(8):
            tv = p.op("tensor", lambda E, c8=c8: E.transpose(
                ctx.pA[:, 512 * 4 + 8 * c8: 512 * 4 + 8 * (c8 + 1)],
                logF[:, 128 * c8:128 * (c8 + 1)], ctx.idf[0:NE, 0:NE]),
                waits=([("V", logv)] + ctx.guard(4)) if c8 == 0 else (),
                inc="T" if c8 == 7 else None)
        vv = p.op("vector", lambda E: E.tensor_copy(
            ltm.rearrange("p c e -> p (c e)"),
            ctx.pA[:, 512 * 4:512 * 4 + 64]), waits=[("T", tv)], inc="V")
        ctx.rguard(4, "V", vv)
        vv = p.op("vector", lambda E: E.tensor_reduce(
            lmax[:, :], ltm[:, :, :], AX.X, ALU.max), waits=[("V", vv)],
            inc="V")
        mv = None
        for c8 in range(8):
            mv = p.op("vector", lambda E, c8=c8: E.tensor_scalar(
                mtm[:, c8, :], ltm[:, c8, :], lmax[:, c8:c8 + 1], None,
                ALU.is_ge), waits=[("V", vv)] if c8 == 0 else (), inc="V")
        tv = None
        for c8 in range(8):
            tv = p.op("tensor", lambda E, c8=c8: E.transpose(
                ctx.pt(6, 128, off=128 * c8)[0:NE, :], mtm[:, c8, :],
                ctx.idb[:]),
                waits=([("V", mv)] + ctx.guard(6)) if c8 == 0 else (),
                inc="T" if c8 == 7 else None)
        maskv = p.op("vector", lambda E: E.tensor_copy(
            mF[:, :], ctx.pt(6)[0:NE, :]), waits=[("T", tv)], inc="V")
        ctx.rguard(6, "V", maskv)
        yac_v = p.op("vector", lambda E: E.memset(
            yac.rearrange("p k t -> p (k t)"), 0.0), inc="V")

        for e in range(NE):
            for tts in range(2):
                tv = p.op("tensor", lambda E, e=e, tts=tts: E.matmul(
                    ctx.pb(4), sel8b[:, 128 * e:128 * (e + 1)],
                    mF[0:NE, 512 * tts:512 * (tts + 1)],
                    start=True, stop=True),
                    waits=([("V", maskv), ("WG", gv)] + ctx.guard(4)),
                    inc="T")
                vm = p.op("vector", lambda E, tts=tts: E.tensor_copy(
                    mbc[:, 512 * tts:512 * (tts + 1)], ctx.pb(4)),
                    waits=[("T", tv)], inc="V")
                ctx.rguard(4, "V", vm)
            mbcv = p.cnt("V")
            hv = None
            for ht in range(KC):
                bk = (ht % 2) * 2
                tv = None
                for tts in range(2):
                    for k in range(KC):
                        w = []
                        if k == 0:
                            w = ctx.guard(bk + tts)
                            if ht == 0 and tts == 0:
                                w = w + [("WG", w1v), ("V", t1_ready)]
                        tv = p.op("tensor", lambda E, k=k, ht=ht, tts=tts,
                                  bk=bk: E.matmul(
                            ctx.pb(bk + tts),
                            w1b[:, k, 128 * ht:128 * (ht + 1)],
                            t1[:, k, 512 * tts:512 * (tts + 1)],
                            start=(k == 0), stop=(k == KC - 1)),
                            waits=w, inc="T" if k == KC - 1 else None)
                sv = p.op("scalar", lambda E, ht=ht, bk=bk, e=e: E.activation(
                    hsb[:, ht, :], ctx.pb(bk, 1024), AF.Gelu,
                    bias=b1s[:, ht, e:e + 1]),
                    waits=[("T", tv), ("IN", iv)], inc="S")
                ctx.rguard(bk, "S", sv)
                ctx.rguard(bk + 1, "S", sv)
                hv = p.op("vector", lambda E, ht=ht: E.tensor_tensor(
                    hsb[:, ht, :], hsb[:, ht, :], mbc[:, :], ALU.mult),
                    waits=[("S", sv), ("V", mbcv)], inc="V")
            nxt = (e + 1) if e < NE - 1 else (0 if th == 0 else None)
            if nxt is not None:
                w1v = load_w("w1", w1b, nxt, [("T", p.cnt("T"))])
            for do in range(KC):
                bk = (do % 2) * 2
                tv = None
                for tts in range(2):
                    for i in range(KC):
                        w = []
                        if i == 0:
                            w = ctx.guard(bk + tts)
                            if do == 0 and tts == 0:
                                w = w + [("V", hv), ("V", yac_v),
                                         ("WG", w2v)]
                        tv = p.op("tensor", lambda E, i=i, do=do, tts=tts,
                                  bk=bk: E.matmul(
                            ctx.pb(bk + tts),
                            w2b[:, i, 128 * do:128 * (do + 1)],
                            hsb[:, i, 512 * tts:512 * (tts + 1)],
                            start=(i == 0), stop=(i == KC - 1)),
                            waits=w, inc="T" if i == KC - 1 else None)
                vv = p.op("vector", lambda E, do=do, bk=bk: E.tensor_tensor(
                    yac[:, do, :], yac[:, do, :], ctx.pb(bk, 1024), ALU.add),
                    waits=[("T", tv)], inc="V")
                ctx.rguard(bk, "V", vv)
                ctx.rguard(bk + 1, "V", vv)
            nxt = (e + 1) if e < NE - 1 else (0 if th == 0 else None)
            if nxt is not None:
                w2v = load_w("w2", w2b, nxt, [("T", p.cnt("T"))])
        yv = p.cnt("V")
        for do in range(KC):
            tvs = []
            for tts in range(2):
                tvs.append(p.op("tensor", lambda E, do=do, tts=tts: E.matmul(
                    ctx.pb(4 + tts), b2b[0:NE, 128 * do:128 * (do + 1)],
                    mF[0:NE, 512 * tts:512 * (tts + 1)],
                    start=True, stop=True),
                    waits=([("V", yv)] + ctx.guard(4 + tts)), inc="T"))
            for tts in range(2):
                a = 1024 * th + 512 * tts
                p.op("vector", lambda E, do=do, tts=tts:
                     E.scalar_tensor_tensor(
                    ctx.f32scr[:, 0:512],
                    ctx.pb(4 + tts), 1.0,
                    yac[:, do, 512 * tts:512 * (tts + 1)],
                    ALU.bypass, ALU.add), waits=[("T", tvs[tts])], inc="V")
                vv = p.op("vector", lambda E, do=do, a=a, tts=tts:
                          E.scalar_tensor_tensor(
                    x[:, do, a:a + 512],
                    ctx.f32scr[:, 0:512], 0.5,
                    x[:, do, a:a + 512], ALU.mult, ALU.add), inc="V")
                ctx.rguard(4 + tts, "V", vv)
    ctx.close_scope(bsc)



def emit_attn(ctx, cut=None):
    p, dram, x = ctx.p, ctx.dram, ctx.x
    bsc = len(ctx._scopes)
    fence0 = p.fence()

    FM = ctx.scoped("FM", [128, T], F32)
    selhf = ctx.scoped("selhf", [NH, NH * 128], F32)
    csxz = ctx.RB[32:33, 2048:6144]
    bwxz = ctx.RB[33:34, 2048:6144]
    csbw = ctx.RB[32:34, 2048:6144]
    gcol = ctx.scoped("gcol", [128, KC], BF16)
    bcol = ctx.scoped("bcol", [128, KC], BF16)
    gcolf = ctx.scoped("gcolf", [128, KC], F32)
    lncols = ctx.scoped("lncols", [128, NT, 1], F32)
    acol = ctx.scoped("acol", [NH, 1], F32)
    nacol = ctx.scoped("nacol", [NH, 1], F32)
    dlt = ctx.scoped("dlt", [128, NT, NH], F32)
    dEs = ctx.scoped("dEs", [128, NT, NH], F32)
    coef = ctx.scoped("coef", [128, NT, NH], F32)
    btm = ctx.scoped("btm", [128, NT, NS], BF16)
    ctm = ctx.scoped("ctm", [128, NT, NS], BF16)
    tmE2 = ctx.scoped("tmE2", [128, NT, 16], F32)
    tmNR = ctx.scoped("tmNR", [128, NT, 16], F32)
    dcol128 = ctx.scoped("dcol128", [128, NH], F32)
    a128 = ctx.scoped("a128", [128, NH], F32)
    na128 = ctx.scoped("na128", [128, NH], F32)
    if getattr(ctx, "want_dbg", False):
        ctx.dbg.update(FM=FM, dlt=dlt, dEs=dEs, coef=coef, btm=btm,
                       ctm=ctm, tmE2=tmE2, tmNR=tmNR, lncols=lncols,
                       a128=a128, RF1=ctx.RF1, RB=ctx.RB)
    psc = len(ctx._scopes)
    lnscr_a = ctx.scoped("lnscr_a", [128, KC, 256], BF16)
    ctx.lnscr = lnscr_a
    relF = FM[0:16, :]
    thetaF = FM[32:64, :]
    dtFv = FM[64:80, :]
    gamFv = FM[96:112, :]
    r1p1_row = ctx.r1p1_row
    mur_row = ctx.mur_row

    emit_ln_stats(ctx, attn=True)
    lnw = ctx.ln_done

    for nm, dst in (("gam", None), ("bet", None)):
        pass
    p.op("gpsimd", lambda E: E.dma_start(
        dram["w_out_bf"], dram["w_out"]), waits=fence0, inc="WG")
    ctx.wout_cast = p.cnt("WG") * 16
    p.op("gpsimd", lambda E: E.dma_start(gcol[:], dram["gam"]), inc="WG")
    p.op("gpsimd", lambda E: E.dma_start(bcol[:], dram["bet"]), inc="WG")
    p.op("sync", lambda E: E.dma_start(gcolf[:], dram["gam"]),
        waits=fence0, inc="IN")
    p.op("sync", lambda E: E.dma_start(acol[:], dram["a_log_col"]), inc="IN")
    p.op("sync", lambda E: E.dma_start(dcol128[:], dram["d128"]), inc="IN")
    p.op("sync", lambda E: E.dma_start(selhf[:, :], dram["selh"]), inc="IN")
    iv_ab = p.cnt("IN") * 16
    sv = p.op("scalar", lambda E: E.activation(nacol[:], acol[:], AF.Exp),
              waits=[("IN", iv_ab)], inc="S")
    av = p.op("vector", lambda E: E.tensor_scalar(
        acol[:], nacol[:], -1.0, None, ALU.mult), waits=[("S", sv)], inc="V")
    tv = p.op("tensor", lambda E: E.transpose(
        ctx.pb(2, NH)[0:1, :], acol[0:NH, 0:1], ctx.idf[0:NH, 0:NH]),
        waits=[("V", av)] + ctx.guard(2), inc="T")
    avv = p.op("vector", lambda E: E.tensor_copy(
        ctx.f32scr[0:1, 384:400], ctx.pb(2, NH)[0:1, :]),
        waits=[("T", tv)], inc="V")
    tv = p.op("tensor", lambda E: E.matmul(
        ctx.pb(2, NH, off=128), ctx.onerf_all[0:1, :],
        ctx.f32scr[0:1, 384:400], start=True, stop=True),
        waits=[("V", avv)], inc="T")
    p.op("vector", lambda E: E.tensor_copy(
        a128[:, :], ctx.pb(2, NH, off=128)), waits=[("T", tv)], inc="V")
    av = p.op("vector", lambda E: E.tensor_scalar(
        na128[:, :], a128[:, :], -1.0, None, ALU.mult), inc="V")
    ctx.rguard(2, "V", av)
    selsc = p.op("vector", lambda E: E.tensor_scalar(
        selhf[:, :], selhf[:, :], nacol[0:NH, 0:1], None, ALU.mult),
        waits=[("IN", iv_ab), ("S", sv)], inc="V")

    vv = p.op("vector", lambda E: E.tensor_copy(
        ctx.tmp_row[0:1, :], ctx.r12_row[0:1, :]), waits=lnw, inc="V")
    for c in range(NT):
        bk = 2 + (c % 2)
        tv = p.op("tensor", lambda E, c=c, bk=bk: E.matmul(
            ctx.pb(bk, 1), ctx.tmp_row[0:1, 128 * c:128 * (c + 1)],
            ctx.onerf_all[64:65, 0:1], start=True, stop=True),
            waits=([("V", vv)] + ctx.guard(bk)), inc="T")
        lv = p.op("vector", lambda E, c=c, bk=bk: E.tensor_copy(
            lncols[:, c, 0:1], ctx.pb(bk, 1)), waits=[("T", tv)], inc="V")
        ctx.rguard(bk, "V", lv)
    lncols_v = p.cnt("V")

    if cut == "p1":
        ctx.close_scope(psc)
        return
    wsm = ctx.scoped("wsm", [128, KC, 160], BF16)
    thwb = ctx.scoped("thwb", [128, KC, 32], BF16)
    smcorr = ctx.scoped("smcorr", [34, 192], BF16)
    thcs = smcorr[32:33, 160:192]
    cs_sm = smcorr[32:33, 0:160]
    bw_sm = smcorr[33:34, 0:160]
    sm2 = smcorr[32:34, 0:160]
    small = ctx.scoped("small", [128, NT, 160], F32)
    dtt = ctx.scoped("dtt", [128, NT, NH], F32)
    gmt = ctx.scoped("gmt", [128, NT, NH], F32)
    alp = ctx.scoped("alp", [128, NT, NH], F32)
    tmAt = ctx.scoped("tmAt", [128, NT, 32], F32)
    tmB = ctx.scoped("tmB", [128, NT, 64], BF16)
    FM2 = ctx.scoped("FM2", [128, T], F32)
    ones64 = FM2[64:80, :]
    cumF = FM2[32:48, :]
    phiF = FM2[64:96, :]
    rscA = FM[32:64, 0:512]
    th64 = FM[32:64, 512:1024]
    sinb = FM[32:64, 1024:1536]
    packF = ctx.scoped("packF", [128, 512], F32)
    packT = ctx.scoped("packT", [64, 512], F32)
    ctx._pf = packF
    bsf = ctx.scoped("bsf", [128, NT, NS], F32)
    csf = ctx.scoped("csf", [128, NT, NS], F32)
    dtb128 = ctx.scoped("dtb128", [128, NH], F32)
    bcB = ctx.scoped("bcB", [128, NS], F32)
    bcC = ctx.scoped("bcC", [128, NS], F32)
    bBt = ctx.scoped("bBt", [128, NS], F32)
    bCt = ctx.scoped("bCt", [128, NS], F32)
    colscr = ctx.scoped("colscr", [128, NT], F32)
    colscr2 = ctx.scoped("colscr2", [128, NT], F32)
    if getattr(ctx, "want_dbg", False):
        ctx.dbg.update(small=small, smcorr=smcorr, bsf=bsf, csf=csf,
                       tmB=tmB, tmAt=tmAt, FM2=FM2, packF=packF, packT=packT)

    p.op("gpsimd", lambda E: E.dma_start(
        wsm[:], dram["w_in"].rearrange("(k p) o -> p k o", p=128)
        [:, :, 4096:4256]), inc="WG")
    p.op("gpsimd", lambda E: E.dma_start(
        thwb[:], dram["w_theta"].rearrange("(k p) o -> p k o", p=128)),
        inc="WG")
    gv_w = p.cnt("WG") * 16
    for nm, dst in (("dt_bias128", dtb128), ("bcB128", bcB), ("bcC128", bcC),
                    ("bB128", bBt), ("bC128", bCt)):
        p.op("sync", lambda E, nm=nm, dst=dst: E.dma_start(dst[:], dram[nm]),
             inc="IN")
    iv_sm = p.cnt("IN") * 16

    for k in range(KC):
        p.op("tensor", lambda E, k=k: E.matmul(
            ctx.prow(2, 160), gcol[:, k:k + 1], wsm[:, k, :],
            start=(k == 0), stop=(k == KC - 1)),
            waits=([("WG", gv_w), ("IN", iv_ab)] + ctx.guard(2))
            if k == 0 else ())
    tv = None
    for k in range(KC):
        tv = p.op("tensor", lambda E, k=k: E.matmul(
            ctx.prow(3, 160), bcol[:, k:k + 1], wsm[:, k, :],
            start=(k == 0), stop=(k == KC - 1)),
            waits=ctx.guard(3) if k == 0 else (),
            inc="T" if k == KC - 1 else None)
    v2 = p.op("vector", lambda E: E.tensor_copy(
        cs_sm[0:1, :], ctx.prow(2, 160)), waits=[("T", tv)], inc="V")
    ctx.rguard(2, "V", v2)
    v3 = p.op("vector", lambda E: E.tensor_copy(
        ctx.stage_b[0:1, 0:160], ctx.prow(3, 160)),
        waits=[("IN", ctx.ir12_dma)], inc="V")
    ctx.rguard(3, "V", v3)
    p.op("sync", lambda E: E.dma_start(
        bw_sm[0:1, :], ctx.stage_b[0:1, 0:160]),
        waits=[("V", v3)], inc="IN")
    sm_dma = p.cnt("IN") * 16
    gsv = None
    for k in range(KC):
        gsv = p.op("vector", lambda E, k=k: E.tensor_scalar(
            wsm[:, k, :], wsm[:, k, :], gcolf[:, k:k + 1], None, ALU.mult),
            waits=[("V", v3), ("IN", p.cnt("IN") * 16)] if k == 0 else (),
            inc="V")

    tv = None
    for k in range(KC):
        tv = p.op("tensor", lambda E, k=k: E.matmul(
            ctx.prow(2, 32), ctx.onec[:, 0:1], thwb[:, k, :],
            start=(k == 0), stop=(k == KC - 1)),
            waits=([("WG", gv_w)] + ctx.guard(2)) if k == 0 else (),
            inc="T" if k == KC - 1 else None)
    thv = p.op("vector", lambda E: E.tensor_copy(
        thcs[0:1, :], ctx.prow(2, 32)), waits=[("T", tv)], inc="V")
    ctx.rguard(2, "V", thv)
    th_tv = 0
    for tt in range(8):
        a = 256 * tt
        sv = None
        for k in range(KC):
            sv = p.op("scalar", lambda E, k=k, a=a: E.copy(
                lnscr_a[:, k, :], x[:, k, a:a + 256]),
                waits=p.fence() if (tt == 0 and k == 0) else (
                    [("T", th_tv)] if k == 0 else ()), inc="S")
        for k in range(KC):
            p.op("tensor", lambda E, k=k: E.matmul(
                ctx.pA[0:32, 512 * 2:512 * 2 + 256], thwb[:, k, :],
                lnscr_a[:, k, :], start=(k == 0), stop=False),
                waits=([("S", sv), ("V", thv)] + ctx.guard(2))
                if k == 0 else ())
        tv = p.op("tensor", lambda E, a=a: E.matmul(
            ctx.pA[0:32, 512 * 2:512 * 2 + 256], thcs[0:1, :],
            ctx.mu_b[0:1, a:a + 256], start=False, stop=True), inc="T")
        th_tv = tv
        vv = p.op("vector", lambda E, a=a: E.tensor_copy(
            thetaF[:, a:a + 256], ctx.pA[0:32, 512 * 2:512 * 2 + 256]),
            waits=[("T", tv)], inc="V")
        ctx.rguard(2, "V", vv)
    thetaF_v = p.cnt("V")

    smv = None
    xbc_war = 0
    for c in range(NT):
        sv = None
        for k in range(KC):
            sv = p.op("scalar", lambda E, k=k, c=c: E.copy(
                ctx.xbc[:, k, :], x[:, k, 128 * c:128 * (c + 1)]),
                waits=[("T", xbc_war)] if k == 0 else (), inc="S")
        for k in range(KC):
            p.op("tensor", lambda E, k=k, c=c: E.matmul(
                ctx.pb(2, 160), ctx.xbc[:, k, :], wsm[:, k, :],
                start=(k == 0), stop=False),
                waits=([("S", sv), ("V", gsv), ("V", v2),
                        ("IN", sm_dma)] + ctx.guard(2))
                if k == 0 else ())
        tv = p.op("tensor", lambda E, c=c: E.matmul(
            ctx.pb(2, 160), ctx.mu2[:, 128 * c:128 * (c + 1)],
            sm2[:, :], start=False, stop=True), inc="T")
        xbc_war = tv
        smv = p.op("scalar", lambda E, c=c: E.activation(
            small[:, c, :], ctx.pb(2, 160), AF.Identity,
            scale=lncols[:, c, 0:1]),
            waits=[("T", tv)] + ([("V", lncols_v)] if c == 0 else []),
            inc="S")
        ctx.rguard(2, "S", smv)

    avv = None
    for c in range(NT):
        avv = p.op("vector", lambda E, c=c: E.tensor_tensor(
            alp[:, c, :], small[:, c, 128:144], dtb128[:, :], ALU.add),
            waits=[("S", smv), ("IN", iv_sm)] if c == 0 else (), inc="V")
    sv = p.op("scalar", lambda E: E.activation(
        dtt.rearrange("p c h -> p (c h)"), alp.rearrange("p c h -> p (c h)"),
        AF.Exp), waits=[("V", avv)], inc="S")
    dt_v = p.op("scalar", lambda E: E.activation(
        dtt.rearrange("p c h -> p (c h)"), dtt.rearrange("p c h -> p (c h)"),
        AF.Ln, bias=1.0), inc="S")
    lam_v = p.op("scalar", lambda E: E.activation(
        gmt[:, :, :], small[:, :, 144:160], AF.Sigmoid),
        waits=[("S", smv)], inc="S")
    vv = p.op("vector", lambda E: E.tensor_tensor(
        alp.rearrange("p c h -> p (c h)"), dtt.rearrange("p c h -> p (c h)"),
        gmt.rearrange("p c h -> p (c h)"), ALU.mult),
        waits=[("S", dt_v), ("S", lam_v)], inc="V")
    gam_v = p.op("vector", lambda E: E.tensor_tensor(
        gmt.rearrange("p c h -> p (c h)"), dtt.rearrange("p c h -> p (c h)"),
        alp.rearrange("p c h -> p (c h)"), ALU.subtract), inc="V")

    for c in range(NT):
        bk = 2 + (c % 2)
        tv = p.op("tensor", lambda E, c=c, bk=bk: E.transpose(
            ctx.pA[0:NH, 512 * bk:512 * bk + 128], dtt[:, c, :],
            ctx.idf[:]),
            waits=([("S", dt_v)] + ctx.guard(bk)), inc="T")
        vv = p.op("vector", lambda E, c=c, bk=bk: E.tensor_copy(
            dtFv[:, 128 * c:128 * (c + 1)],
            ctx.pA[0:NH, 512 * bk:512 * bk + 128]),
            waits=[("T", tv)], inc="V")
        ctx.rguard(bk, "V", vv)
        bk = 2 + ((c + 1) % 2)
        tv = p.op("tensor", lambda E, c=c, bk=bk: E.transpose(
            ctx.pA[0:NH, 512 * bk:512 * bk + 128], gmt[:, c, :],
            ctx.idf[:]),
            waits=([("V", gam_v)] + ctx.guard(bk)), inc="T")
        vv = p.op("vector", lambda E, c=c, bk=bk: E.tensor_copy(
            gamFv[:, 128 * c:128 * (c + 1)],
            ctx.pA[0:NH, 512 * bk:512 * bk + 128]),
            waits=[("T", tv)], inc="V")
        ctx.rguard(bk, "V", vv)
    dgF_v = p.cnt("V")

    if cut == "p2":
        ctx.close_scope(psc)
        return
    vv = p.op("vector", lambda E: E.memset(ones64[:, :], 1.0), inc="V")
    cum_v = p.op("vector", lambda E: E.tensor_tensor_scan(
        cumF[:, :], ones64[0:NH, :], dtFv[:, :], 0.0, ALU.mult, ALU.add),
        waits=[("V", dgF_v), ("V", vv)], inc="V")
    rel_v = p.op("vector", lambda E: E.tensor_copy(
        relF[:, 0:128], cumF[:, 0:128]), waits=[("V", cum_v)], inc="V")
    for c in range(1, NT):
        rel_v = p.op("vector", lambda E, c=c: E.tensor_scalar(
            relF[:, 128 * c:128 * (c + 1)], cumF[:, 128 * c:128 * (c + 1)],
            cumF[:, 128 * c - 1:128 * c], None, ALU.subtract), inc="V")


    dtbv = []
    for tt in range(8):
        vv = p.op("vector", lambda E, tt=tt: E.tensor_copy(
            lnscr_a[0:NH, tt, 0:256], dtFv[:, 256 * tt:256 * (tt + 1)]),
            waits=[("V", dgF_v)], inc="V")
        dtbv.append(vv)
    for tt in range(8):
        a = 256 * tt
        tv = p.op("tensor", lambda E, tt=tt: E.matmul(
            ctx.prow(2, 256), ctx.onec[0:NH, 0:1],
            lnscr_a[0:NH, tt, 0:256],
            start=True, stop=True),
            waits=([("V", dtbv[tt])] + ctx.guard(2)), inc="T")
        vv = p.op("vector", lambda E, a=a: E.tensor_scalar(
            ctx.tmp_row[0:1, a:a + 256], ctx.prow(2, 256), 1.0 / NH, None,
            ALU.mult), waits=[("T", tv)], inc="V")
        ctx.rguard(2, "V", vv)
    dts_v = None
    for a in range(0, T, 512):
        p.op("vector", lambda E, a=a: E.tensor_copy(
            ctx.f32scr[64:65, 0:512], ctx.r_row[0:1, a:a + 512]), inc="V")
        dts_v = p.op("vector", lambda E, a=a: E.tensor_tensor(
            ctx.tmp_row[0:1, a:a + 512], ctx.tmp_row[0:1, a:a + 512],
            ctx.f32scr[64:65, 0:512], ALU.mult), inc="V")
    for tt in range(4):
        a = 512 * tt
        tv = p.op("tensor", lambda E, a=a: E.matmul(
            ctx.pA[0:32, 512 * 2:512 * 2 + 512],
            ctx.onerf_all[64:65, 0:32],
            ctx.tmp_row[0:1, a:a + 512], start=True, stop=True),
            waits=([("V", dts_v)] + ctx.guard(2)), inc="T")
        vv = p.op("vector", lambda E, a=a: E.tensor_tensor(
            thetaF[:, a:a + 512], thetaF[:, a:a + 512],
            ctx.pA[0:32, 512 * 2:512 * 2 + 512], ALU.mult),
            waits=[("T", tv), ("V", thetaF_v)], inc="V")
        ctx.rguard(2, "V", vv)
    if cut == "p2b":
        ctx.close_scope(psc)
        return
    ones32 = FM2[32:64, :]
    vv = p.op("vector", lambda E: E.memset(ones32[:, :], 1.0), inc="V")
    phi_v = p.op("vector", lambda E: E.tensor_tensor_scan(
        phiF[:, :], ones32[:, :], thetaF[:, :], 0.0, ALU.mult, ALU.add),
        inc="V")

    if cut == "p3":
        ctx.close_scope(psc)
        return
    ES = [0] * NT
    pk_war = 0
    for tt in range(4):
        a = 512 * tt
        e2 = p.op("scalar", lambda E, a=a: E.activation(
            packF[0:NH, :], relF[:, a:a + 512], AF.Exp,
            scale=acol[0:NH, 0:1]),
            waits=[("V", rel_v), ("V", av), ("T", pk_war)], inc="S")
        nr = p.op("vector", lambda E, a=a: E.tensor_scalar(
            packF[32:48, :], relF[:, a:a + 512], acol[0:NH, 0:1], None,
            ALU.mult), waits=[("V", rel_v), ("V", av), ("T", pk_war)],
            inc="V")
        if tt < 3:
            g1 = p.op("vector", lambda E, a=a: E.tensor_copy(
                packF[64:80, :], gamFv[:, a + 1:a + 513]),
                waits=[("V", dgF_v), ("V", p.cnt("V"))], inc="V")
        else:
            p.op("vector", lambda E, a=a: E.tensor_copy(
                packF[64:80, 0:511], gamFv[:, a + 1:a + 512]),
                waits=[("V", dgF_v), ("V", p.cnt("V"))], inc="V")
            g1 = p.op("vector", lambda E: E.memset(
                packF[64:80, 511:512], 0.0), inc="V")
        for cc in range(4):
            c = 4 * tt + cc
            p.op("vector", lambda E, c=c, cc=cc, a=a: E.tensor_scalar(
                packT[0:16, 128 * cc:128 * (cc + 1)],
                relF[:, 128 * c:128 * (c + 1)],
                relF[:, 128 * c + 127:128 * c + 128], None, ALU.subtract),
                waits=[("V", rel_v), ("T", pk_war)] if cc == 0 else (),
                inc="V")
        es = p.op("scalar", lambda E: E.activation(
            packF[96:112, :], packT[0:16, :], AF.Exp,
            scale=nacol[0:NH, 0:1]), waits=[("V", p.cnt("V"))], inc="S")
        if cut == "p3a":
            continue
        wv = p.op("vector", lambda E, a=a: E.tensor_copy(
            th64[:, :], phiF[:, a:a + 512]),
            waits=[("V", phi_v), ("S", es)], inc="V")
        wv = p.op("vector", lambda E: E.tensor_scalar(
            sinb[:, :], th64[:, :], 1.0 / TWO_PI, RMAGIC,
            ALU.mult, ALU.add), inc="V")
        wv = p.op("vector", lambda E: E.tensor_scalar(
            sinb[:, :], sinb[:, :], RMAGIC, None, ALU.subtract), inc="V")
        wv = p.op("vector", lambda E: E.scalar_tensor_tensor(
            sinb[:, :], sinb[:, :], -TWO_PI, th64[:, :],
            ALU.mult, ALU.add), inc="V")
        wv = p.op("vector", lambda E: E.tensor_scalar(
            sinb[:, :], sinb[:, :], -math.pi, math.pi, ALU.max, ALU.min),
            inc="V")
        s1 = p.op("scalar", lambda E: E.activation(
            packT[32:64, :], sinb[:, :], AF.Sin), waits=[("V", wv)],
            inc="S")
        wv = p.op("vector", lambda E: E.tensor_scalar(
            rscA[:, :], th64[:, :], 1.0 / TWO_PI, 0.25,
            ALU.mult, ALU.add), inc="V")
        wv = p.op("vector", lambda E: E.tensor_scalar(
            rscA[:, :], rscA[:, :], RMAGIC, None, ALU.add), inc="V")
        wv = p.op("vector", lambda E: E.tensor_scalar(
            rscA[:, :], rscA[:, :], RMAGIC, None, ALU.subtract), inc="V")
        wv = p.op("vector", lambda E: E.tensor_scalar(
            th64[:, :], th64[:, :], math.pi / 2.0, None,
            ALU.add), inc="V")
        wv = p.op("vector", lambda E: E.scalar_tensor_tensor(
            rscA[:, :], rscA[:, :], -TWO_PI, th64[:, :],
            ALU.mult, ALU.add), inc="V")
        wv = p.op("vector", lambda E: E.tensor_scalar(
            rscA[:, :], rscA[:, :], -math.pi, math.pi, ALU.max, ALU.min),
            inc="V")
        s2 = p.op("scalar", lambda E: E.activation(
            packT[0:32, :], rscA[:, :], AF.Sin), waits=[("V", wv)], inc="S")
        if cut == "p3b":
            continue
        for cc in range(4):
            c = 4 * tt + cc
            bk = 2 + (c % 2)
            tv = p.op("tensor", lambda E, cc=cc, bk=bk: E.transpose(
                ctx.pb(bk, 128), packF[:, 128 * cc:128 * (cc + 1)],
                ctx.idf[:]),
                waits=([("V", g1), ("S", e2), ("S", es), ("V", nr)]
                       + ctx.guard(bk)), inc="T")
            tv2 = p.op("tensor", lambda E, cc=cc, bk=bk: E.transpose(
                ctx.pb(bk, 64, off=128), packT[:, 128 * cc:128 * (cc + 1)],
                ctx.idf[0:64, 0:64]),
                waits=[("S", s1), ("S", s2)], inc="T")
            v1 = p.op("vector", lambda E, c=c, bk=bk: E.tensor_copy(
                tmE2[:, c, :], ctx.pb(bk, 16)),
                waits=[("T", tv2)], inc="V")
            p.op("vector", lambda E, c=c, bk=bk: E.tensor_copy(
                tmNR[:, c, :], ctx.pb(bk, 16, off=32)), inc="V")
            p.op("vector", lambda E, c=c, bk=bk: E.tensor_copy(
                tmAt[:, c, 0:16], ctx.pb(bk, 16, off=64)), inc="V")
            p.op("vector", lambda E, c=c, bk=bk: E.tensor_copy(
                tmAt[:, c, 16:32], ctx.pb(bk, 16, off=96)), inc="V")
            v2 = p.op("vector", lambda E, c=c, bk=bk: E.tensor_copy(
                tmB[:, c, :], ctx.pb(bk, 64, off=128)),
                waits=[("T", tv2)], inc="V")
            ctx.rguard(bk, "V", v2)
            ES[c] = v2
            pk_war = tv2
    tmAB_v = p.cnt("V")

    if cut in ("p4", "p3a", "p3b"):
        ctx.close_scope(psc)
        return
    dl_v = de_v = None
    for c in range(NT):
        dl_v = p.op("vector", lambda E, c=c: E.tensor_tensor(
            dlt[:, c, :], alp[:, c, :], tmAt[:, c, 0:16], ALU.add),
            waits=[("V", ES[c])], inc="V")
    for c in range(NT):
        de_v = p.op("vector", lambda E, c=c: E.tensor_tensor(
            dEs[:, c, :], dlt[:, c, :], tmAt[:, c, 16:32], ALU.mult),
            waits=[("V", ES[NT - 1])] if c == 0 else (), inc="V")

    msq = ctx.scoped("msq", [128, NT, 2], F32)
    rsq = ctx.scoped("rsq", [128, NT, 2], F32)
    vv = None
    for g in range(4):
        for j in range(8):
            c = 4 * g + j // 2
            off = 64 * (j % 2)
            vv = p.op("vector", lambda E, c=c, off=off, j=j: E.tensor_tensor(
                ctx.f32scr[:, 64 * j:64 * (j + 1)],
                small[:, c, off:off + 64], small[:, c, off:off + 64],
                ALU.mult),
                waits=[("S", smv), ("IN", iv_sm)] if (g == 0 and j == 0)
                else (), inc="V")
        vv = p.op("vector", lambda E, g=g: E.tensor_reduce(
            msq[:, 4 * g:4 * (g + 1), :].rearrange("p a b -> p (a b)"),
            ctx.f32scr[:, 0:512].rearrange("p (a b) -> p a b", b=64),
            AX.X, ALU.add), waits=[("V", vv)], inc="V")
    vv = p.op("vector", lambda E: E.tensor_scalar(
        rsq.rearrange("p a b -> p (a b)"),
        msq.rearrange("p a b -> p (a b)"), 1.0 / NS, EPS2,
        ALU.mult, ALU.add), waits=[("V", vv)], inc="V")
    p.op("scalar", lambda E: E.activation(
        rsq.rearrange("p a b -> p (a b)"),
        rsq.rearrange("p a b -> p (a b)"), AF.Ln),
        waits=[("V", vv)], inc="S")
    rs_v = p.op("scalar", lambda E: E.activation(
        rsq.rearrange("p a b -> p (a b)"),
        rsq.rearrange("p a b -> p (a b)"), AF.Exp, scale=-0.5), inc="S")

    bc_v = None
    for c in range(NT):
        for (oi, off, sc_t, bi_t, dst) in ((0, 0, bcB, bBt, bsf),
                                           (1, 64, bcC, bCt, csf)):
            vv = p.op("vector", lambda E, c=c, off=off, oi=oi, sc_t=sc_t:
                      E.scalar_tensor_tensor(
                ctx.f32scr[:, 128:192], small[:, c, off:off + 64],
                rsq[:, c, oi:oi + 1], sc_t[:, :], ALU.mult, ALU.mult),
                waits=[("S", rs_v)] if c == 0 and oi == 0 else (), inc="V")
            vv = p.op("vector", lambda E, c=c, bi_t=bi_t, dst=dst:
                      E.tensor_tensor(
                dst[:, c, :], ctx.f32scr[:, 128:192], bi_t[:, :], ALU.add),
                inc="V")
        for (src, dst) in ((bsf, btm), (csf, ctm)):
            p.op("vector", lambda E, c=c, src=src: E.tensor_tensor(
                ctx.f32scr[:, 0:32], src[:, c, 0:32], tmB[:, c, 0:32],
                ALU.mult), inc="V")
            p.op("vector", lambda E, c=c, src=src: E.tensor_tensor(
                ctx.f32scr[:, 32:64], src[:, c, 32:64], tmB[:, c, 32:64],
                ALU.mult), inc="V")
            p.op("vector", lambda E, c=c, dst=dst: E.tensor_tensor(
                dst[:, c, 0:32], ctx.f32scr[:, 0:32], ctx.f32scr[:, 32:64],
                ALU.subtract), inc="V")
            p.op("vector", lambda E, c=c, src=src: E.tensor_tensor(
                ctx.f32scr[:, 0:32], src[:, c, 0:32], tmB[:, c, 32:64],
                ALU.mult), inc="V")
            p.op("vector", lambda E, c=c, src=src: E.tensor_tensor(
                ctx.f32scr[:, 32:64], src[:, c, 32:64], tmB[:, c, 0:32],
                ALU.mult), inc="V")
            bc_v = p.op("vector", lambda E, c=c, dst=dst: E.tensor_tensor(
                dst[:, c, 32:64], ctx.f32scr[:, 0:32], ctx.f32scr[:, 32:64],
                ALU.add), inc="V")
    for g in range(2):
        for j in range(8):
            c = 8 * g + j
            vv = p.op("vector", lambda E, c=c, j=j: E.tensor_tensor(
                ctx.f32scr[:, 64 * j:64 * (j + 1)], bsf[:, c, :],
                csf[:, c, :], ALU.mult), inc="V")
        vv = p.op("vector", lambda E, g=g: E.tensor_reduce(
            colscr[:, 8 * g:8 * (g + 1)],
            ctx.f32scr[:, 0:512].rearrange("p (a b) -> p a b", b=64),
            AX.X, ALU.add), waits=[("V", vv)], inc="V")
    vv = p.op("vector", lambda E: E.tensor_scalar(
        colscr2[:, :], colscr[:, :], -1.0, None, ALU.mult),
        waits=[("V", vv)], inc="V")
    for c in range(NT):
        cf_v = p.op("vector", lambda E, c=c: E.scalar_tensor_tensor(
            coef[:, c, :], tmAt[:, c, 0:16], colscr2[:, c:c + 1],
            dcol128[:, :], ALU.mult, ALU.add),
            waits=[("V", vv), ("V", ES[c])] if c == 0 else
            [("V", ES[c])], inc="V")
    prep_done = p.fence()
    ctx.close_scope(psc)
    if cut == "prep":
        return

    msc = len(ctx._scopes)
    winb = ctx.scoped("winb", [128, KC, 2 * DI], BF16)
    xsil = ctx.scoped("xsil", [128, GRP, DI], BF16)
    zsil = ctx.scoped("zsil", [128, GRP, DI], BF16)
    gyF = ctx.scoped("gyF", [128, NH, 128], BF16)
    if getattr(ctx, "want_dbg", False):
        ctx.dbg.update(gyF=gyF)
    _dbg_main = getattr(ctx, "want_dbg", False)
    woutb = ctx.scoped("woutb", [128, 2, D], BF16)
    s_f = FM[64:128, :].rearrange("n (h d) -> n h d", d=HD)
    s_bb = ctx.scoped("s_bb", [NS, 2, HD], BF16)
    dec_bc = ctx.f32scr[64:64 + NS, 400:416]
    deccol = ctx.scoped("deccol", [NH, 1], F32)
    decrow = ctx.f32scr[0:1, 384:400]
    bFc = ctx.scoped("bFc", [NS, 128], BF16)
    cFc = ctx.scoped("cFc", [NS, 128], BF16)
    rmsb = ctx.scoped("rmsb", [128, 128], BF16)
    wsb = ctx.scoped("wsb", [128, 2, 128], BF16)
    gsb = ctx.scoped("gsb", [128, 2, 128], BF16)
    ceF = ctx.scoped("ceF", [NS, 2, 128], BF16)
    cetm = ctx.scoped("cetm", [128, 2, NS], BF16)
    bstb = ctx.scoped("bstb", [128, 2, NS], BF16)
    gytm = ctx.scoped("gytm", [128, 2, 128], BF16)
    yscr = ctx.f32scr[:, 0:128]
    if _dbg_main:
        ctx.dbg.update(wsb=wsb, gsb=gsb, rmsb=rmsb, ceF=ceF, s_bb=s_bb,
                       selhf=selhf, bstb=bstb, cetm=cetm)

    wk = []
    for k in range(KC):
        p.op("gpsimd", lambda E, k=k: E.dma_start(
            winb[:, k, :],
            dram["w_in"][128 * k:128 * (k + 1), 0:4096]),
            waits=prep_done if k == 0 else (), inc="WG")
        wk.append(p.cnt("WG") * 16)

    cw_v = None
    for cb in range(8):
        for k in range(KC):
            p.op("tensor", lambda E, k=k, cb=cb: E.matmul(
                ctx.prow(2), gcol[:, k:k + 1],
                winb[:, k, 512 * cb:512 * (cb + 1)],
                start=(k == 0), stop=(k == KC - 1)),
                waits=([("WG", wk[-1])] + ctx.guard(2) + prep_done)
                if (cb == 0 and k == 0) else (ctx.guard(2) if k == 0 else ()))
        tv = None
        for k in range(KC):
            tv = p.op("tensor", lambda E, k=k, cb=cb: E.matmul(
                ctx.prow(3), bcol[:, k:k + 1],
                winb[:, k, 512 * cb:512 * (cb + 1)],
                start=(k == 0), stop=(k == KC - 1)),
                waits=ctx.guard(3) if k == 0 else (),
                inc="T" if k == KC - 1 else None)
        v2 = p.op("vector", lambda E, cb=cb: E.tensor_copy(
            csxz[0:1, 512 * cb:512 * (cb + 1)], ctx.prow(2)),
            waits=[("T", tv)], inc="V")
        ctx.rguard(2, "V", v2)
        cw_v = p.op("vector", lambda E, cb=cb: E.tensor_copy(
            ctx.stage_b[0:1, 2048 + 512 * cb:2048 + 512 * (cb + 1)],
            ctx.prow(3)), inc="V")
        ctx.rguard(3, "V", cw_v)
    p.op("sync", lambda E: E.dma_start(
        bwxz[0:1, :], ctx.stage_b[0:1, 2048:6144]),
        waits=[("V", cw_v)], inc="IN")
    xz_dma = p.cnt("IN") * 16
    gs_v = None
    for k in range(KC):
        gs_v = p.op("vector", lambda E, k=k: E.tensor_scalar(
            winb[:, k, :], winb[:, k, :], gcolf[:, k:k + 1], None,
            ALU.mult),
            waits=[("V", cw_v)] if k == 0 else (), inc="V")
    sinit = p.op("vector", lambda E: E.memset(
        s_f.rearrange("n h d -> n (h d)"), 0.0), waits=prep_done, inc="V")
    mr_v = None
    for a in range(0, T, 512):
        p.op("vector", lambda E, a=a: E.tensor_copy(
            ctx.f32scr[0:1, 0:512], ctx.r_row[0:1, a:a + 512]),
            waits=prep_done if a == 0 else (), inc="V")
        mr_v = p.op("vector", lambda E, a=a: E.tensor_tensor(
            mur_row[0:1, a:a + 512], ctx.mu_row[0:1, a:a + 512],
            ctx.f32scr[0:1, 0:512], ALU.mult), inc="V")
    rp_v = p.op("vector", lambda E: E.tensor_scalar(
        r1p1_row[0:1, :], ctx.r_row[0:1, :], 1.0, None, ALU.add), inc="V")
    row_ov = [("V", rp_v)]

    wout_next = None

    def load_wout(di, tdep):
        buf = di % 2
        p.op("sync", lambda E, di=di, buf=buf: E.dma_start(
            woutb[:, buf, :],
            dram["w_out_bf"][128 * di:128 * (di + 1), :]),
            waits=list(tdep) + [("WG", ctx.wout_cast)], inc="IN")
        return p.cnt("IN") * 16

    silu_done = {}
    xz_war = 0
    tslot_guard = [0, 0, 0, 0]
    gv_par = [0, 0]
    cet_par = [0, 0]
    tvy_par = [0, 0]
    tvs_par = [0, 0]
    gyt_par = [0, 0]
    for g in range(NT // GRP):
        for ci in range(GRP):
            c = GRP * g + ci
            sv = None
            for k in range(KC):
                sv = p.op("scalar", lambda E, k=k, c=c: E.copy(
                    ctx.xbc[:, k, :], x[:, k, 128 * c:128 * (c + 1)]),
                    waits=[("T", xz_war)] if k == 0 else (), inc="S")
            for cb in range(8):
                bk = cb % 2
                tv = None
                for k in range(KC):
                    w = []
                    if k == 0:
                        w = ctx.guard(bk) + [("WG", wk[KC - 1]),
                                             ("V", gs_v), ("S", sv),
                                             ("IN", xz_dma),
                                             ("IN", ctx.ir12_dma)]
                    tv = p.op("tensor", lambda E, k=k, c=c, cb=cb, bk=bk:
                              E.matmul(
                        ctx.pb(bk),
                        ctx.xbc[:, k, :],
                        winb[:, k, 512 * cb:512 * (cb + 1)],
                        start=(k == 0), stop=False), waits=w)
                tv = p.op("tensor", lambda E, c=c, cb=cb, bk=bk: E.matmul(
                    ctx.pb(bk), ctx.mu2[:, 128 * c:128 * (c + 1)],
                    csbw[:, 512 * cb:512 * (cb + 1)],
                    start=False, stop=True), inc="T")
                xz_war = tv
                dst = xsil if cb < 4 else zsil
                o = (cb % 4) * 512
                sv2 = p.op("scalar", lambda E, c=c, ci=ci, cb=cb, bk=bk,
                           dst=dst, o=o: E.activation(
                    dst[:, ci, o:o + 512], ctx.pb(bk), AF.Silu,
                    scale=lncols[:, c, 0:1]), waits=[("T", tv)], inc="S")
                ctx.rguard(bk, "S", sv2)
                silu_done[(g, ci)] = sv2

        for ci in range(GRP if cut != "inproj" else 0):
            c = GRP * g + ci
            tb = p.op("tensor", lambda E, c=c: E.transpose(
                ctx.pt(6, 128, off=0)[0:NS, :], btm[:, c, :], ctx.idb[:]),
                waits=ctx.guard(6) + prep_done, inc="T")
            tc = p.op("tensor", lambda E, c=c: E.transpose(
                ctx.pt(6, 128, off=128)[0:NS, :], ctm[:, c, :], ctx.idb[:]),
                inc="T")
            vb = p.op("vector", lambda E: E.tensor_copy(
                bFc[:, :], ctx.pt(6, 128, off=0)[0:NS, :]),
                waits=[("T", tc)], inc="V")
            vc = p.op("vector", lambda E: E.tensor_copy(
                cFc[:, :], ctx.pt(6, 128, off=128)[0:NS, :]), inc="V")
            ctx.rguard(6, "V", vc)
            tv = p.op("tensor", lambda E: E.matmul(
                ctx.pb(0, 128), bFc[0:NS, :], cFc[0:NS, :],
                start=True, stop=True),
                waits=[("V", vc)] + ctx.guard(0), inc="T")
            rv = p.op("vector", lambda E: E.tensor_tensor(
                rmsb[:, :], ctx.pb(0, 128), ctx.tri[:, :], ALU.mult),
                waits=[("T", tv)], inc="V")
            ctx.rguard(0, "V", rv)
            dv = p.op("vector", lambda E, c=c: E.tensor_scalar(
                deccol[0:NH, 0:1], relF[:, 128 * c + 127:128 * c + 128],
                acol[0:NH, 0:1], None, ALU.mult),
                waits=row_ov, inc="V")
            sv = p.op("scalar", lambda E: E.activation(
                deccol[0:NH, 0:1], deccol[0:NH, 0:1], AF.Exp),
                waits=[("V", dv)], inc="S")
            tv = p.op("tensor", lambda E: E.transpose(
                ctx.pb(4, NH)[0:1, :], deccol[0:NH, 0:1],
                ctx.idf[0:NH, 0:NH]),
                waits=[("S", sv)] + ctx.guard(4), inc="T")
            vv = p.op("vector", lambda E: E.tensor_copy(
                decrow[0:1, :], ctx.pb(4, NH)[0:1, :]), waits=[("T", tv)],
                inc="V")
            tv = p.op("tensor", lambda E: E.matmul(
                ctx.pb(4, NH, off=128)[0:NS, :], ctx.onerf_all[0:1, 0:NS],
                decrow[0:1, :], start=True, stop=True),
                waits=[("V", vv)], inc="T")
            dbv = p.op("vector", lambda E: E.tensor_copy(
                dec_bc[:, :], ctx.pb(4, NH, off=128)[0:NS, :]),
                waits=[("T", tv)], inc="V")
            ctx.rguard(4, "V", dbv)

            for h in range(NH):
                pg = h % 2
                bkD = 2 + pg
                bkY = 0 + pg
                bkS = 4 + pg
                tv = p.op("tensor", lambda E, h=h, c=c, bkD=bkD: E.matmul(
                    ctx.pb(bkD, 128), selhf[:, 128 * h:128 * (h + 1)],
                    relF[:, 128 * c:128 * (c + 1)],
                    start=True, stop=True),
                    waits=ctx.guard(bkD) + ([("V", rel_v), ("V", selsc)]
                                            if h == 0 else []), inc="T")
                sv = p.op("scalar", lambda E, h=h, c=c, bkD=bkD:
                          E.activation(
                    ctx.f32scr[:, 256:384], ctx.pb(bkD, 128), AF.Relu,
                    bias=tmNR[:, c, h:h + 1]),
                    waits=[("T", tv)], inc="S")
                ctx.rguard(bkD, "S", sv)
                sv = p.op("scalar", lambda E, pg=pg: E.activation(
                    wsb[:, pg, :], ctx.f32scr[:, 256:384], AF.Exp,
                    scale=-1.0), waits=[("V", gv_par[pg])], inc="S")
                gv_h = p.op("vector", lambda E, h=h, c=c, pg=pg:
                            E.scalar_tensor_tensor(
                    gsb[:, pg, :], wsb[:, pg, :], dlt[:, c, h:h + 1],
                    rmsb[:, :], ALU.mult, ALU.mult),
                    waits=[("S", sv), ("V", rv), ("T", tvy_par[pg])],
                    inc="V")
                gv_par[pg] = gv_h
                sl = (2 * h) % 4
                cv = p.op("vector", lambda E, h=h, c=c, pg=pg:
                          E.tensor_scalar(
                    cetm[:, pg, :], ctm[:, c, :],
                    tmE2[:, c, h:h + 1], None, ALU.mult),
                    waits=[("T", cet_par[pg])], inc="V")
                ce_t = p.op("tensor", lambda E, sl=sl, pg=pg: E.transpose(
                    ctx.pt(6 + sl // 2, 128, off=128 * (sl % 2))[0:NS, :],
                    cetm[:, pg, :], ctx.idb[:]),
                    waits=[("V", cv), ("V", tslot_guard[sl])], inc="T")
                cet_par[pg] = ce_t
                cev = p.op("vector", lambda E, sl=sl, pg=pg: E.tensor_copy(
                    ceF[:, pg, :],
                    ctx.pt(6 + sl // 2, 128, off=128 * (sl % 2))[0:NS, :]),
                    waits=[("T", ce_t), ("T", tvy_par[pg])], inc="V")
                tslot_guard[sl] = cev
                bst_v = p.op("vector", lambda E, h=h, c=c, pg=pg:
                             E.tensor_scalar(
                    bstb[:, pg, :], btm[:, c, :], dEs[:, c, h:h + 1],
                    None, ALU.mult), waits=[("T", tvs_par[pg])], inc="V")
                sbv = p.op("vector", lambda E, h=h, pg=pg: E.tensor_copy(
                    s_bb[:, pg, :], s_f[:, h, :]),
                    waits=[("V", sinit), ("T", tvy_par[pg])], inc="V")
                w = ctx.guard(bkY) + [("V", cev), ("V", sbv),
                                      ("S", silu_done[(g, ci)])]
                p.op("tensor", lambda E, bkY=bkY, pg=pg: E.matmul(
                    ctx.pb(bkY, 128), ceF[:, pg, :], s_bb[:, pg, :],
                    start=True, stop=False), waits=w)
                tvy = p.op("tensor", lambda E, h=h, ci=ci, bkY=bkY, pg=pg:
                           E.matmul(
                    ctx.pb(bkY, 128), gsb[:, pg, :],
                    xsil[:, ci, 128 * h:128 * (h + 1)],
                    start=False, stop=True),
                    waits=[("V", gv_h)], inc="T")
                tvy_par[pg] = tvy
                tvs = p.op("tensor", lambda E, h=h, ci=ci, bkS=bkS, pg=pg:
                           E.matmul(
                    ctx.pb(bkS, 128)[0:NS, :], bstb[:, pg, :],
                    xsil[:, ci, 128 * h:128 * (h + 1)],
                    start=True, stop=True),
                    waits=ctx.guard(bkS) + [("V", bst_v)], inc="T")
                tvs_par[pg] = tvs
                vv = p.op("vector", lambda E, h=h, c=c, ci=ci, bkY=bkY:
                          E.scalar_tensor_tensor(
                    yscr[:, :], xsil[:, ci, 128 * h:128 * (h + 1)],
                    coef[:, c, h:h + 1], ctx.pb(bkY, 128),
                    ALU.mult, ALU.add), waits=[("T", tvy)], inc="V")
                ctx.rguard(bkY, "V", vv)
                sv_up = p.op("vector", lambda E, h=h, bkS=bkS:
                             E.scalar_tensor_tensor(
                    s_f[:, h, :], s_f[:, h, :], dec_bc[:, h:h + 1],
                    ctx.pb(bkS, 128)[0:NS, :], ALU.mult, ALU.add),
                    waits=[("T", tvs), ("V", dbv)], inc="V")
                ctx.rguard(bkS, "V", sv_up)
                gy_v = p.op("vector", lambda E, h=h, ci=ci, pg=pg:
                            E.tensor_tensor(
                    gytm[:, pg, :], yscr[:, :],
                    zsil[:, ci, 128 * h:128 * (h + 1)], ALU.mult),
                    waits=[("T", gyt_par[pg])], inc="V")
                sl = (2 * h + 1) % 4
                gy_t = p.op("tensor", lambda E, sl=sl, pg=pg: E.transpose(
                    ctx.pt(6 + sl // 2, 128, off=128 * (sl % 2)),
                    gytm[:, pg, :], ctx.idb[:]),
                    waits=[("V", gy_v), ("V", tslot_guard[sl])], inc="T")
                gyt_par[pg] = gy_t
                gyv = p.op("vector", lambda E, h=h, ci=ci, sl=sl:
                           E.tensor_copy(
                    gyF[:, h, 128 * ci:128 * (ci + 1)],
                    ctx.pt(6 + sl // 2, 128, off=128 * (sl % 2))),
                    waits=[("T", gy_t)], inc="V")
                tslot_guard[sl] = gyv
        ctx.rguard(6, "V", p.cnt("V"))
        ctx.rguard(7, "V", p.cnt("V"))
        gy_done = p.cnt("V")
        if cut in ("inproj", "scan"):
            continue

        c = g
        if wout_next is None:
            wout_next = load_wout(0, [("T", p.cnt("T"))])
        po_done = {}
        t_marks = {}
        for di in range(NH):
            w = [("IN", wout_next)]
            if di == 0:
                w += [("V", gy_done)] + ctx.guard(0, 1)
            for do in range(KC):
                bk = do // 4
                off = (do % 4) * 128
                po_done[do] = p.op("tensor", lambda E, di=di, do=do, bk=bk,
                                   off=off: E.matmul(
                    ctx.pb(bk, 128, off=off),
                    woutb[:, di % 2, 128 * do:128 * (do + 1)],
                    gyF[:, di, :],
                    start=(di == 0 and do % 4 == 0), stop=(di == NH - 1),
                    skip_group_check=True),
                    waits=w if do == 0 else (),
                    inc="T" if do == KC - 1 else None)
                w = ()
            t_marks[di] = p.cnt("T")
            if di < NH - 1:
                wout_next = load_wout(
                    di + 1, [("T", t_marks[di - 1])] if di >= 1 else ())
        tv = p.op("tensor", lambda E, c=c: E.matmul(
            ctx.pb(4, 128), ctx.onerf_all[64:65, :],
            r1p1_row[0:1, 128 * c:128 * (c + 1)], start=True, stop=True),
            waits=ctx.guard(4), inc="T")
        tv = p.op("tensor", lambda E, c=c: E.matmul(
            ctx.pb(5, 128), ctx.onerf_all[0:1, :],
            mur_row[0:1, 128 * c:128 * (c + 1)], start=True, stop=True),
            waits=ctx.guard(5), inc="T")
        for do in range(KC):
            bk = do // 4
            off = (do % 4) * 128
            a = 128 * c
            p.op("vector", lambda E, do=do, a=a: E.tensor_tensor(
                ctx.f32scr[:, 128:256], x[:, do, a:a + 128],
                ctx.pb(4, 128), ALU.mult),
                waits=[("T", tv), ("T", po_done[do])], inc="V")
            p.op("vector", lambda E: E.tensor_tensor(
                ctx.f32scr[:, 128:256], ctx.f32scr[:, 128:256],
                ctx.pb(5, 128), ALU.subtract), inc="V")
            vv = p.op("vector", lambda E, do=do, a=a, bk=bk, off=off:
                      E.tensor_tensor(
                x[:, do, a:a + 128], ctx.f32scr[:, 128:256],
                ctx.pb(bk, 128, off=off), ALU.add), inc="V")
            ctx.rguard(bk, "V", vv)
        ctx.rguard(4, "V", p.cnt("V"))
        ctx.rguard(5, "V", p.cnt("V"))
        wout_next = load_wout(0, [("T", p.cnt("T"))]) if g < NT - 1 \
            else None
    ctx.close_scope(msc)



_NC_CACHE = {}


def _get_nc(phases):
    key = tuple(phases)
    if key not in _NC_CACHE:
        _NC_CACHE[key] = build_nc(phases)
    return _NC_CACHE[key]


def _sel(n):
    m = np.zeros((n, n * 128), np.float32)
    for i in range(n):
        m[i, i * 128:(i + 1) * 128] = 1.0
    return m


def make_inmaps(inputs):
    f = np.float32
    u = np.ascontiguousarray(inputs["L0_attn_u"], dtype=f)
    rep = lambda v, n: np.ascontiguousarray(
        np.broadcast_to(np.asarray(v, f).reshape(1, -1), (128, n)))
    shared = dict(
        w_in=np.ascontiguousarray(inputs["L0_attn_in_proj_w"], f),
        w_theta=np.ascontiguousarray(inputs["L0_attn_theta_w"], f),
        w_out=np.ascontiguousarray(inputs["L0_attn_out_proj_w"], f),
        wg=np.ascontiguousarray(inputs["L0_ffn_Wg"], f),
        bg_col=np.ascontiguousarray(inputs["L0_ffn_bg"], f).reshape(NE, 1),
        w1=np.ascontiguousarray(inputs["L0_ffn_W1"], f),
        w2=np.ascontiguousarray(inputs["L0_ffn_W2"], f),
        b1T=np.ascontiguousarray(np.transpose(inputs["L0_ffn_b1"]), f),
        b2=np.ascontiguousarray(inputs["L0_ffn_b2"], f),
        gam=np.ascontiguousarray(
            np.asarray(inputs["L0_attn_norm_scale"], f).reshape(KC, 128).T),
        bet=np.ascontiguousarray(
            np.asarray(inputs["L0_attn_norm_bias"], f).reshape(KC, 128).T),
        a_log_col=np.ascontiguousarray(
            inputs["L0_attn_A_log"], f).reshape(NH, 1),
        dt_bias128=rep(inputs["L0_attn_dt_bias"], NH),
        d128=rep(inputs["L0_attn_D"], NH),
        bcB128=rep(inputs["L0_attn_bcB_scale"], NS),
        bcC128=rep(inputs["L0_attn_bcC_scale"], NS),
        bB128=rep(inputs["L0_attn_B_bias"], NS),
        bC128=rep(inputs["L0_attn_C_bias"], NS),
        selh=_sel(NH),
        sel8=_sel(NE),
        ident_f=np.eye(128, dtype=f),
        ident_b2=np.eye(128, dtype=f),
        tri=np.ascontiguousarray(np.tri(128, 128, 0, dtype=f).T),
        ones_col=np.ones((128, 1), f),
        ones_row=np.ones((1, 128), f),
        ones65=np.ones((65, 128), f),
    )
    in_maps = []
    for b in range(NCORES):
        m = dict(shared)
        m["xT"] = np.ascontiguousarray(u[b].T)
        in_maps.append(m)
    return in_maps


def kernel(**inputs):
    nc = _get_nc(("ffn1", "attn", "ffn2"))
    in_maps = make_inmaps(inputs)
    res = run_bass_kernel_spmd(nc, in_maps, core_ids=list(range(NCORES)))
    out = np.stack([np.ascontiguousarray(r["out"].T) for r in res.results])
    return out.astype(np.float32)
